# version 02
# baseline (speedup 1.0000x reference)
import contextlib
import numpy as np

import concourse.bass as bass
import concourse.mybir as mybir
from concourse.bass_utils import run_bass_kernel_spmd

F32 = mybir.dt.float32
AF = mybir.ActivationFunctionType
ALU = mybir.AluOpType
AX = mybir.AxisListType

T, E, H, KV, D, WIN, DF = 2048, 2048, 16, 4, 128, 512, 8192
NQ, QT, RING = 4, 512, 1024
NCORES = 8
f32 = np.float32

(ST_NEGM, ST_NEGMC, ST_LNL, ST_BIAS2, ST_LVEC, ST_TMP, ST_TMP2, ST_TMP3,
 ST_SUMY0, ST_SUMY1, ST_SUMY2, ST_SUMY3, ST_SQ0, ST_SQ1, ST_SQ2, ST_SQ3,
 ST_MU, ST_NEGMU, ST_MSQ, ST_VAR, ST_SD, ST_R, ST_GMAX, ST_GMIN,
 ST_GA, ST_GB, ST_GM, ST_GAM, ST_I128, ST_GD128, ST_RCP, ST_X31) = range(32)
(SC_BPOS, SC_BNEG, SC_C, SC_SQ, SC_THRF, SC_NTHRF, SC_GAMF,
 SC_THRP, SC_NTHRP, SC_GAMP, SC_T0, SC_T1, SC_T2, SC_T3, SC_T4, SC_T5) = range(32, 48)


def host_consts():
    c = np.zeros((128, 960), dtype=f32)
    c[:, 0:128] = np.eye(128, dtype=f32)
    i = np.arange(128)
    m1 = np.where(i[None, :] <= i[:, None], f32(-1e9), f32(0.0)).astype(f32)
    m2 = np.where(i[None, :] > i[:, None], f32(-1e9), f32(0.0)).astype(f32)
    c[:, 128:256] = m1
    c[:, 256:384] = m2
    c[:, 384:512] = m1.T
    c[:, 512:640] = m2.T
    c[0, 640:768] = 1.0
    c[:, 768] = 1.0
    return c


def host_rope():
    inv_freq = (1.0 / 10000.0 ** (np.arange(0, D, 2, dtype=f32) / f32(D))).astype(f32)
    ang = (np.arange(T, dtype=f32)[:, None] * inv_freq[None, :]).astype(f32)
    cos = np.cos(ang).astype(f32)
    sin = np.sin(ang).astype(f32)
    out = np.zeros((T, 256), dtype=f32)
    out[:, 0:64] = cos
    out[:, 64:128] = cos
    out[:, 128:192] = sin
    out[:, 192:256] = sin
    return out


class Sem:
    def __init__(self, stack, nc, name):
        self.h = stack.enter_context(nc.semaphore(name))
        self.n = 0

    def bump(self, k=1):
        self.n += k
        return self.n


class Rec:

    ENGS = ("tensor", "vector", "scalar", "gpsimd", "sync")

    def __init__(self, block):
        self.block = block
        self.ops = {e: [] for e in self.ENGS}

    def add(self, eng, fn):
        self.ops[eng].append(fn)

    def flush(self):
        for eng in self.ENGS:
            lst = self.ops[eng]
            if not lst:
                continue
            self.ops[eng] = []

            def body(e, _lst=tuple(lst)):
                for f in _lst:
                    f(e)
            getattr(self.block, eng)(body)


class KB:

    def __init__(self, nc, dbg=()):
        self.nc = nc
        self.dbg = set(dbg)
        self.st = contextlib.ExitStack()
        st = self.st

        hb = {}
        hb["x"] = nc.dram_tensor("x", [T, E], F32, kind="ExternalInput").ap()
        hb["wqT"] = nc.dram_tensor("wqT", [E, H * D], F32, kind="ExternalInput").ap()
        hb["wkT"] = nc.dram_tensor("wkT", [E, KV * D], F32, kind="ExternalInput").ap()
        hb["wvT"] = nc.dram_tensor("wvT", [E, KV * D], F32, kind="ExternalInput").ap()
        hb["woT"] = nc.dram_tensor("woT", [H * D, E], F32, kind="ExternalInput").ap()
        hb["wfcT"] = nc.dram_tensor("wfcT", [E, DF], F32, kind="ExternalInput").ap()
        hb["wpjT"] = nc.dram_tensor("wpjT", [DF, E], F32, kind="ExternalInput").ap()
        hb["consts"] = nc.dram_tensor("consts", [128, 960], F32, kind="ExternalInput").ap()
        hb["rope"] = nc.dram_tensor("rope", [T, 256], F32, kind="ExternalInput").ap()
        hb["y"] = nc.dram_tensor("y", [T, E], F32, kind="ExternalOutput").ap()
        hb["sfc"] = nc.dram_tensor("sfc_scr", [E, DF], F32, kind="Internal").ap()
        hb["spj"] = nc.dram_tensor("spj_scr", [DF, E], F32, kind="Internal").ap()
        self.hb = hb
        self.dbg_hbm = {}
        for name, cols in (("dbg_q", 8192), ("dbg_k", 4096), ("dbg_v", 4096),
                           ("dbg_xt", 8192), ("dbg_ot", 8192), ("dbg_y1", 8192),
                           ("dbg_g", 16384), ("dbg_st", 64), ("dbg_pt", 4096),
                           ("dbg_xh", 4096), ("dbg_gel", 16384),
                           ("dbg_d1", 8192), ("dbg_st1", 64), ("dbg_xt1", 8192),
                           ("dbg_at5", 4096), ("dbg_ats", 64)):
            if name in self.dbg:
                self.dbg_hbm[name] = nc.dram_tensor(name, [128, cols], F32, kind="ExternalOutput").ap()

        def sb(name, cols):
            return st.enter_context(nc.sbuf_tensor(name, [128, cols], F32))

        self.cons = sb("cons", 960)
        self.stt = sb("stt", 64)
        self.ropeq = sb("ropeq", 1024)
        self.k_sb = sb("k_sb", KV * RING)
        self.v_sb = sb("v_sb", 8 * 512)
        self.r1 = sb("r1", 8192)
        self.r2 = sb("r2", 8192)
        self.r3 = sb("r3", 16384)
        self.r4 = sb("r4", 8192)
        self.r5 = sb("r5", 4096)

        self.pS = st.enter_context(nc.psum_tensor("pS", [128, 1024], F32))
        self.pb = [st.enter_context(nc.psum_tensor(f"pb{i}", [128, 512], F32)) for i in range(6)]

        self.block = st.enter_context(nc.Block())
        self.rec = Rec(self.block)

        self.sems = {}
        for name in ("dW dA dG dR bar vF sF tnV tnG "
                     "aTx aVx aMM aRp aSt aTq aVe "
                     "b1 bmax be1 blnl bbv bbt bbw bmm2 bmsk2 be2 bpv bov "
                     "cmm cev cl1 cl2 "
                     "d1v d1s dxp dxe dmm dg dst dgm dmm2 dz d6v d6s").split():
            self.sems[name] = Sem(st, nc, "s_" + name)

        self.ident = self.cons[:, 0:128]
        self.m1 = self.cons[:, 128:256]
        self.m2 = self.cons[:, 256:384]
        self.m1t = self.cons[:, 384:512]
        self.m2t = self.cons[:, 512:640]
        self.onesr = self.cons[0:1, 640:768]
        self.onesc = self.cons[:, 768:769]

    def op(self, eng, meth, *args, waits=(), incs=(), **kw):
        def f(e):
            for sem, val in waits:
                e.wait_ge(sem.h, val)
            ins = getattr(e, meth)(*args, **kw)
            for sem, k in incs:
                ins.then_inc(sem.h, k)
        self.rec.add(eng, f)

    def wait(self, eng, sem, val):
        def f(e):
            e.wait_ge(sem.h, val)
        self.rec.add(eng, f)

    def inc(self, sem, k=1):
        sem.bump(k)
        return (sem, k)

    def fop(self, eng, meth, *args, waits=(), incs=(), **kw):
        incs = list(incs)
        if incs:
            self.op(eng, meth, *args, waits=waits, incs=incs, **kw)
            sem = incs[0][0]
            self.wait(eng, sem, sem.n)
        else:
            f = self.sems["vF" if eng == "vector" else "sF"]
            self.op(eng, meth, *args, waits=waits, incs=[self.inc(f)], **kw)
            self.wait(eng, f, f.n)

    def barrier(self):
        self.rec.flush()
        bar = self.sems["bar"]
        tgt = bar.n + 5
        bar.n = tgt
        for eng in Rec.ENGS:
            def f(e, _t=tgt):
                e.nop().then_inc(bar.h, 1)
                e.wait_ge(bar.h, _t)
            self.rec.add(eng, f)
        self.rec.flush()

    def col(self, c):
        return self.stt[:, c:c + 1]

    def dump(self, name, src_ap):
        if name not in self.dbg_hbm:
            return
        self.barrier()
        dG = self.sems["dG"]
        self.op("scalar", "dma_start", incs=[self.inc(dG, 16)],
                out=self.dbg_hbm[name][:, :], in_=src_ap)
        self.wait("scalar", dG, dG.n)
        self.barrier()

    def build(self, skip_tern=False):
        self.emit_setup()
        if not skip_tern:
            self.emit_ternarize("fc")
            self.emit_ternarize("pj")
        self.tern_dG = self.sems["dG"].n
        for Q in range(NQ):
            self.emit_qkv(Q)
            self.barrier()
            if Q == 0:
                self.dump("dbg_xt", self.r1[:, :])
                self.dump("dbg_q", self.r2[:, :])
                self.dump("dbg_k", self.k_sb[:, :])
                self.dump("dbg_v", self.v_sb[:, :])
            self.emit_attn(Q)
            self.barrier()
            if Q == 0:
                self.dump("dbg_ot", self.r3[:, 0:8192])
            self.emit_wo_ln1(Q)
            self.barrier()
            if Q == 0:
                self.dump("dbg_y1", self.r2[:, :])
            for Eh in range(2):
                self.emit_ffn(Q, Eh)
                self.barrier()
                if Q == 0 and Eh == 0:
                    self.dump("dbg_g", self.r3[:, :])
        self.emit_finish()
        self.rec.flush()

    def emit_setup(self):
        s = self.sems
        dG, d1v, d1s = s["dG"], s["d1v"], s["d1s"]
        if self.dbg:
            for buf in (self.k_sb, self.v_sb, self.r1, self.r2, self.r3, self.stt):
                self.op("vector", "memset", buf[:, :], 0.0)
            self.barrier()
        self.op("sync", "dma_start", incs=[self.inc(dG, 16)],
                out=self.cons[:], in_=self.hb["consts"][:, :])
        w = [(dG, dG.n)]
        c = self.col
        self.op("vector", "memset", c(SC_BPOS), 128.0, waits=w)
        self.fop("vector", "tensor_single_scalar", c(SC_BPOS), c(SC_BPOS), 1e-6, op=ALU.subtract)
        self.fop("vector", "tensor_single_scalar", c(SC_BNEG), c(SC_BPOS), -1.0, op=ALU.mult)
        self.op("vector", "memset", c(SC_SQ), 128.0, incs=[self.inc(d1v)])
        self.op("scalar", "activation", c(SC_SQ), c(SC_SQ), AF.Sqrt,
                waits=[(d1v, d1v.n)], incs=[self.inc(d1s)])
        self.fop("vector", "reciprocal", c(SC_C), c(SC_SQ), waits=[(d1s, d1s.n)])
        self.fop("vector", "tensor_single_scalar", c(SC_T5), c(SC_SQ), -1.0, op=ALU.mult)
        self.barrier()

    def emit_ternarize(self, which):
        s = self.sems
        dW, dG, tnV, tnG = s["dW"], s["dG"], s["tnV"], s["tnG"]
        aTx, aVx = s["aTx"], s["aVx"]
        c = self.col
        if which == "fc":
            w_hbm, s_hbm, rows, cols = self.hb["wfcT"], self.hb["sfc"], E, DF
            thr, nthr, gam, acc = SC_THRF, SC_NTHRF, SC_GAMF, SC_T0
        else:
            w_hbm, s_hbm, rows, cols = self.hb["wpjT"], self.hb["spj"], DF, E
            thr, nthr, gam, acc = SC_THRP, SC_NTHRP, SC_GAMP, SC_T1
        cw = 4096 if cols >= 4096 else cols
        chunks = [(rb, h) for rb in range(rows // 128) for h in range(cols // cw)]
        nch = len(chunks)

        dW0, v0 = dW.n, tnV.n
        self.op("vector", "memset", c(acc), 0.0)
        for i, (rb, h) in enumerate(chunks):
            wt = [] if i < 2 else [(tnV, v0 + i - 1)]
            self.op("sync", "dma_start", waits=wt, incs=[self.inc(dW, 16)],
                    out=self.r4[:, (i % 2) * 4096:(i % 2) * 4096 + cw],
                    in_=w_hbm[rb * 128:rb * 128 + 128, h * cw:h * cw + cw])
            self.fop("vector", "reduce_sum", c(ST_TMP),
                     self.r4[:, (i % 2) * 4096:(i % 2) * 4096 + cw], axis=AX.X,
                     apply_absolute_value=True,
                     waits=[(dW, dW0 + 16 * min(i + 2, nch))])
            self.op("vector", "tensor_tensor", c(acc), c(acc), c(ST_TMP), op=ALU.add,
                    incs=[self.inc(tnV)])
        self.op("tensor", "matmul", self.pb[4][0:1, 0:1], self.onesc, c(acc),
                start=True, stop=True, waits=[(tnV, tnV.n)], incs=[self.inc(aTx)])
        inv_n = 1.0 / (rows * cols)
        self.fop("vector", "tensor_single_scalar", self.stt[0:1, SC_T2:SC_T2 + 1],
                 self.pb[4][0:1, 0:1], inv_n, op=ALU.mult, waits=[(aTx, aTx.n)])
        self.fop("vector", "tensor_scalar", self.stt[0:1, SC_T3:SC_T3 + 1],
                 self.stt[0:1, SC_T2:SC_T2 + 1], 1e-6, 0.5, op0=ALU.add, op1=ALU.mult)
        self.fop("vector", "tensor_single_scalar", self.stt[0:1, SC_T4:SC_T4 + 1],
                 self.stt[0:1, SC_T3:SC_T3 + 1], -1.0, op=ALU.mult, incs=[self.inc(aVx)])
        self.op("tensor", "matmul", self.pb[5][:, 0:3], self.onesr,
                self.stt[0:1, SC_T2:SC_T2 + 3], start=True, stop=True,
                waits=[(aVx, aVx.n)], incs=[self.inc(aTx)])
        self.op("vector", "tensor_copy", c(gam), self.pb[5][:, 0:1], waits=[(aTx, aTx.n)])
        self.op("vector", "tensor_copy", c(thr), self.pb[5][:, 1:2])
        self.op("vector", "tensor_copy", c(nthr), self.pb[5][:, 2:3])
        self.barrier()

        dW1, dG0 = dW.n, dG.n
        done = []
        for i, (rb, h) in enumerate(chunks):
            eng = "gpsimd" if i % 3 == 2 else "vector"
            sem = tnG if eng == "gpsimd" else tnV
            wsrc = self.r4[:, (i % 2) * 4096:(i % 2) * 4096 + cw]
            if eng == "vector":
                a2, b2 = self.r1[:, 0:cw], self.r1[:, 4096:4096 + cw]
            else:
                a2, b2 = self.r2[:, 0:cw], self.r2[:, 4096:4096 + cw]
            what = self.r3[:, (i % 2) * 4096:(i % 2) * 4096 + cw]
            wt = [] if i < 2 else [done[i - 2]]
            self.op("sync", "dma_start", waits=wt, incs=[self.inc(dW, 16)],
                    out=wsrc, in_=w_hbm[rb * 128:rb * 128 + 128, h * cw:h * cw + cw])
            wt = [(dW, dW1 + 16 * min(i + 2, nch))]
            if i >= 2:
                wt.append((dG, dG0 + 16 * i))
            self.op(eng, "tensor_scalar", a2, wsrc, c(thr), c(gam),
                    op0=ALU.is_gt, op1=ALU.mult, waits=wt)
            self.op(eng, "tensor_scalar", b2, wsrc, c(nthr), c(gam),
                    op0=ALU.is_lt, op1=ALU.mult)
            self.op(eng, "tensor_tensor", what, a2, b2, op=ALU.subtract,
                    incs=[self.inc(sem)])
            done.append((sem, sem.n))
            self.op("scalar", "dma_start", waits=[done[i]], incs=[self.inc(dG, 16)],
                    out=s_hbm[rb * 128:rb * 128 + 128, h * cw:h * cw + cw], in_=what)
        self.barrier()

    def emit_qkv(self, Q):
        s = self.sems
        dA, dW = s["dA"], s["dW"]
        aTx, aVx, aMM, aRp, aSt, aTq, aVe = (s[k] for k in
                                             ("aTx", "aVx", "aMM", "aRp", "aSt", "aTq", "aVe"))
        r1, r2, r4, r5 = self.r1, self.r2, self.r4, self.r5
        pb = self.pb
        hb = self.hb

        dA0 = dA.n
        dG = self.sems["dG"]
        self.wait("scalar", dG, dG.n)
        self.wait("vector", dG, dG.n)
        dR = self.sems["dR"]
        for tc in range(4):
            self.op("scalar", "dma_start", incs=[self.inc(dR, 16)],
                    out=self.ropeq[:, tc * 256:tc * 256 + 256],
                    in_=hb["rope"][Q * QT + tc * 128:Q * QT + tc * 128 + 128, :])
        ropeR = dR.n
        aT0, aV0 = aTx.n, aVx.n
        for tc in range(4):
            wt = [] if tc < 2 else [(aTx, aT0 + 16 * (tc - 1))]
            self.op("scalar", "dma_start", waits=wt, incs=[self.inc(dA, 16)],
                    out=r5[:, (tc % 2) * 2048:(tc % 2) * 2048 + 2048],
                    in_=hb["x"][Q * QT + tc * 128:Q * QT + tc * 128 + 128, :])
        k = 0
        for tc in range(4):
            for ec in range(16):
                wt = [(dA, dA0 + 16 * min(tc + 2, 4))]
                if k >= 2:
                    wt.append((aVx, aV0 + k - 1))
                self.op("tensor", "transpose", pb[k % 2][:, 0:128],
                        r5[:, (tc % 2) * 2048 + ec * 128:(tc % 2) * 2048 + ec * 128 + 128],
                        self.ident, waits=wt, incs=[self.inc(aTx)])
                self.op("vector", "tensor_copy",
                        r1[:, ec * 512 + tc * 128:ec * 512 + tc * 128 + 128],
                        pb[k % 2][:, 0:128], waits=[(aTx, aTx.n)], incs=[self.inc(aVx)])
                k += 1
        self.barrier()
        if Q == 0:
            self.dump("dbg_xt1", r1[:, :])

        halves = ([("q", hb["wqT"], sq) for sq in range(8)] +
                  [("k", hb["wkT"], sk) for sk in range(2)] +
                  [("v", hb["wvT"], sv) for sv in range(2)])
        dW0 = dW.n
        aM0, aR0, aS0, aQ0, aE0 = aMM.n, aRp.n, aSt.n, aTq.n, aVe.n
        tile = 0
        for gi, (kind, src, sidx) in enumerate(halves):
            wt = [] if gi < 2 else [(aRp, aR0 + 4 * (gi - 1))]
            self.op("sync", "dma_start", waits=wt, incs=[self.inc(dW, 16)],
                    out=r4[:, (gi % 2) * 4096:(gi % 2) * 4096 + 4096].rearrange(
                        "p (c o) -> p c o", c=16),
                    in_=src.rearrange("(c p) o -> p c o", p=128)[:, :, sidx * 256:sidx * 256 + 256])
            strip = r4[:, (gi % 2) * 4096:(gi % 2) * 4096 + 4096]
            self.wait("tensor", dW, dW0 + 16 * min(gi + 2, len(halves)))
            if gi >= 1:
                self.wait("tensor", aRp, aR0 + 4 * gi)
            for ec in range(16):
                for tc in range(4):
                    last = (ec == 15)
                    self.op("tensor", "matmul", pb[tc][:, 0:256],
                            r1[:, ec * 512 + tc * 128:ec * 512 + tc * 128 + 128],
                            strip[:, ec * 256:ec * 256 + 256],
                            start=(ec == 0), stop=last,
                            incs=[self.inc(aMM)] if last else [])
            for tc in range(4):
                mm_t = aM0 + gi * 4 + tc + 1
                if kind == "v":
                    slot = (Q * 4 + tc) % 8
                    self.op("vector", "tensor_copy",
                            self.v_sb[:, slot * 512 + sidx * 256:slot * 512 + sidx * 256 + 256],
                            pb[tc][:, 0:256], waits=[(aMM, mm_t)], incs=[self.inc(aRp)])
                    continue
                pq = pb[tc][:, 0:256].rearrange("p (h x) -> p h x", h=2)
                lo, hi = pq[:, :, 0:64], pq[:, :, 64:128]
                cosr = self.ropeq[:, tc * 256:tc * 256 + 128].rearrange("p (h x) -> p h x", h=2)
                sinr = self.ropeq[:, tc * 256 + 128:tc * 256 + 256].rearrange("p (h x) -> p h x", h=2)
                stg = r5[:, (tile % 2) * 256:(tile % 2) * 256 + 256]
                stgv = stg.rearrange("p (h x) -> p h x", h=2)
                t1 = r5[:, 512:640].rearrange("p (h x) -> p h x", h=2)
                t2 = r5[:, 640:768].rearrange("p (h x) -> p h x", h=2)
                wt = [(aMM, mm_t), (self.sems["dR"], ropeR)]
                if tile >= 2:
                    wt.append((aTq, aQ0 + 2 * (tile - 1)))
                self.op("vector", "tensor_tensor", t1, lo, cosr, op=ALU.mult, waits=wt)
                self.op("vector", "tensor_tensor", t2, hi, sinr, op=ALU.mult)
                self.op("vector", "tensor_tensor", stgv[:, :, 0:64], t1, t2, op=ALU.subtract)
                self.op("vector", "tensor_tensor", t1, hi, cosr, op=ALU.mult)
                self.op("vector", "tensor_tensor", t2, lo, sinr, op=ALU.mult,
                        incs=[self.inc(aRp)])
                self.op("vector", "tensor_tensor", stgv[:, :, 64:128], t1, t2, op=ALU.add,
                        incs=[self.inc(aSt)])
                for hbk in range(2):
                    ktr = 2 * tile + hbk
                    wt = [(aSt, aS0 + tile + 1)]
                    if ktr >= 2:
                        wt.append((aVe, aE0 + ktr - 1))
                    self.op("tensor", "transpose", pb[4 + ktr % 2][:, 0:128],
                            stg[:, hbk * 128:hbk * 128 + 128], self.ident,
                            waits=wt, incs=[self.inc(aTq)])
                    if kind == "q":
                        hglob = sidx * 2 + hbk
                        dst = r2[:, hglob * 512 + tc * 128:hglob * 512 + tc * 128 + 128]
                    else:
                        kvh = sidx * 2 + hbk
                        rpos = (Q % 2) * 512 + tc * 128
                        dst = self.k_sb[:, kvh * RING + rpos:kvh * RING + rpos + 128]
                    self.op("vector", "tensor_copy", dst, pb[4 + ktr % 2][:, 0:128],
                            waits=[(aTq, aTq.n)], incs=[self.inc(aVe)])
                tile += 1

    def emit_attn(self, Q):
        s = self.sems
        (b1, bmax, be1, blnl, bbv, bbt, bbw, bmm2, bmsk2, be2, bpv, bov) = (
            s[k] for k in ("b1", "bmax", "be1", "blnl", "bbv", "bbt", "bbw",
                           "bmm2", "bmsk2", "be2", "bpv", "bov"))
        dA = s["dA"]
        r2, r3, r5 = self.r2, self.r3, self.r5
        pS, pb = self.pS, self.pb
        c = self.col
        scrP = r5[:, 0:640]
        PT = [r5[:, 640:1280], r5[:, 1280:1920]]
        brow = [r5[0:1, 1920:2048], r5[0:1, 2048:2176]]

        for tc in range(4):
            self.op("scalar", "dma_start", incs=[self.inc(dA, 16)],
                    out=self.r4[:, tc * 2048:tc * 2048 + 2048],
                    in_=self.hb["x"][Q * QT + tc * 128:Q * QT + tc * 128 + 128, :])

        i0 = {k: s[k].n for k in ("b1", "bmax", "be1", "blnl", "bbv", "bbt", "bbw", "bpv", "bov")}
        c20, cm0, ce0 = bmm2.n, bmsk2.n, be2.n
        c2 = 0
        for i, (h, qt) in enumerate([(h, qt) for h in range(H) for qt in range(4)]):
            kvh = h // 4
            qs = Q * QT + qt * 128
            win_lo = max(0, qs - WIN)
            L = qs - win_lo
            qtile = r2[:, h * 512 + qt * 128:h * 512 + qt * 128 + 128]

            segs = []
            if L > 0:
                r0 = win_lo % RING
                p1 = min(RING - r0, L)
                segs.append((0, r0, p1))
                if p1 < L:
                    segs.append((p1, 0, L - p1))
            wt = [(be1, i0["be1"] + i)] if i > 0 else []
            first = True
            for si, (off, rpos, ln) in enumerate(segs):
                self.op("tensor", "matmul", pS[:, off:off + ln], qtile,
                        self.k_sb[:, kvh * RING + rpos:kvh * RING + rpos + ln],
                        start=first, stop=False, waits=wt)
                wt = []
                first = False
            dpos = qs % RING
            self.op("tensor", "matmul", pS[:, L:L + 128], qtile,
                    self.k_sb[:, kvh * RING + dpos:kvh * RING + dpos + 128],
                    start=(L == 0 or L == WIN), stop=False, waits=wt)
            if L == WIN:
                self.op("tensor", "matmul", pS[:, 0:128], self.ident, self.m1,
                        start=False, stop=True)
                self.op("tensor", "matmul", pS[:, L:L + 128], self.ident, self.m2,
                        start=False, stop=True, incs=[self.inc(b1)])
            else:
                self.op("tensor", "matmul", pS[:, L:L + 128], self.ident, self.m2,
                        start=False, stop=True, incs=[self.inc(b1)])
            self.wait("vector", b1, i0["b1"] + i + 1)
            self.fop("vector", "reduce_max", c(ST_NEGM), pS[:, 0:L + 128], axis=AX.X, negate=True)
            self.op("vector", "tensor_tensor", c(ST_NEGMC), c(ST_NEGM), c(SC_C), op=ALU.mult,
                    incs=[self.inc(bmax)])
            self.wait("scalar", bmax, i0["bmax"] + i + 1)
            self.op("scalar", "activation", scrP[:, 0:L + 128], pS[:, 0:L + 128], AF.Exp,
                    bias=c(ST_NEGMC), scale=c(SC_C), accum_out=c(ST_LVEC),
                    incs=[self.inc(be1)])
            self.wait("scalar", be1, be1.n)
            self.op("scalar", "activation", c(ST_LNL), c(ST_LVEC), AF.Ln, incs=[self.inc(blnl)])
            self.wait("vector", blnl, i0["blnl"] + i + 1)
            self.op("vector", "scalar_tensor_tensor", c(ST_BIAS2), c(ST_LNL), c(SC_T5),
                    c(ST_NEGM), op0=ALU.mult, op1=ALU.add, incs=[self.inc(bbv)])
            wt = [(bbv, i0["bbv"] + i + 1)]
            if i >= 2:
                wt.append((bbw, i0["bbw"] + i - 1))
            self.op("tensor", "transpose", pb[4 + i % 2][0:1, 0:128], c(ST_BIAS2),
                    self.ident, waits=wt, incs=[self.inc(bbt)])
            self.op("vector", "tensor_copy", brow[i % 2], pb[4 + i % 2][0:1, 0:128],
                    waits=[(bbt, i0["bbt"] + i + 1)], incs=[self.inc(bbw)])

            nch = L // 128 + 1
            for ct in range(nch):
                kpos = win_lo + ct * 128
                rpos = kpos % RING
                bank = pb[c2 % 2]
                wt = [(bbw, i0["bbw"] + i + 1)]
                if c2 >= 2:
                    wt.append((be2, ce0 + c2 - 1))
                msk = None
                if ct == nch - 1:
                    msk = self.m2t
                elif ct == 0 and L == WIN:
                    msk = self.m1t
                self.op("tensor", "matmul", bank[:, 0:128],
                        self.k_sb[:, kvh * RING + rpos:kvh * RING + rpos + 128],
                        qtile, start=True, stop=False, waits=wt)
                if msk is not None:
                    self.op("tensor", "matmul", bank[:, 0:128], self.ident, msk,
                            start=False, stop=False)
                self.op("tensor", "matmul", bank[:, 0:128], self.onesr, brow[i % 2],
                        start=False, stop=True, incs=[self.inc(bmm2)])
                wt = [(bmm2, c20 + c2 + 1)]
                if ct == 0 and i >= 2:
                    wt.append((bpv, i0["bpv"] + i - 1))
                self.op("scalar", "activation", PT[i % 2][:, ct * 128:ct * 128 + 128],
                        bank[:, 0:128], AF.Exp, scale=c(SC_C),
                        waits=wt, incs=[self.inc(be2)])
                slot = (kpos // 128) % 8
                wt = [(be2, ce0 + c2 + 1)]
                if ct == 0 and i >= 2:
                    wt.append((bov, i0["bov"] + i - 1))
                self.op("tensor", "matmul", pb[2 + i % 2][:, 0:128],
                        self.v_sb[:, slot * 512 + kvh * 128:slot * 512 + kvh * 128 + 128],
                        PT[i % 2][:, ct * 128:ct * 128 + 128],
                        start=(ct == 0), stop=(ct == nch - 1), waits=wt,
                        incs=[self.inc(bpv)] if ct == nch - 1 else [])
                c2 += 1
            self.op("vector", "tensor_copy",
                    r3[:, h * 512 + qt * 128:h * 512 + qt * 128 + 128],
                    pb[2 + i % 2][:, 0:128],
                    waits=[(bpv, i0["bpv"] + i + 1)], incs=[self.inc(bov)])
            if Q == 0 and i == 0:
                self.dump("dbg_at5", r5[:, :])
                self.dump("dbg_ats", self.stt[:, :])

    def emit_wo_ln1(self, Q):
        s = self.sems
        dW, dA, cmm, cev, cl1, cl2 = (s[k] for k in ("dW", "dA", "cmm", "cev", "cl1", "cl2"))
        r2, r3, r4, r5 = self.r2, self.r3, self.r4, self.r5
        pb = self.pb
        c = self.col
        dW0, cm0, ce0 = dW.n, cmm.n, cev.n
        widx = 0
        for eg in range(4):
            for hd in range(16):
                wt = []
                if widx >= 2:
                    wt.append((cmm, cm0 + widx - 1))
                self.op("sync", "dma_start", waits=wt, incs=[self.inc(dW, 16)],
                        out=r5[:, (widx % 2) * 512:(widx % 2) * 512 + 512],
                        in_=self.hb["woT"][hd * 128:hd * 128 + 128, eg * 512:eg * 512 + 512])
                wt = [(dW, dW0 + 16 * min(widx + 2, 64))]
                if hd == 0 and eg > 0:
                    wt.append((cev, ce0 + 4 * eg))
                for tc in range(4):
                    self.op("tensor", "matmul", pb[tc][:, 0:512],
                            r3[:, hd * 512 + tc * 128:hd * 512 + tc * 128 + 128],
                            r5[:, (widx % 2) * 512:(widx % 2) * 512 + 512],
                            start=(hd == 0), stop=(hd == 15),
                            waits=wt if tc == 0 else [],
                            incs=[self.inc(cmm)] if tc == 3 else [])
                widx += 1
            self.wait("vector", cmm, cm0 + widx)
            self.wait("vector", dA, dA.n)
            for tc in range(4):
                self.op("vector", "scalar_tensor_tensor",
                        r2[:, tc * 2048 + eg * 512:tc * 2048 + eg * 512 + 512],
                        pb[tc][:, 0:512], 1.0,
                        r4[:, tc * 2048 + eg * 512:tc * 2048 + eg * 512 + 512],
                        op0=ALU.mult, op1=ALU.add, incs=[self.inc(cev)])
        for tc in range(4):
            y1 = r2[:, tc * 2048:tc * 2048 + 2048]
            self.op("scalar", "activation", r5[:, 2048:4096], y1, AF.Square,
                    accum_out=c(ST_SQ0 + tc), waits=[(cev, cev.n)], incs=[self.inc(cl2)])
            self.fop("vector", "reduce_sum", c(ST_TMP3), y1, axis=AX.X,
                     waits=[(cev, cev.n)])
            self.wait("vector", cl2, cl2.n)
            self.op("vector", "tensor_single_scalar", c(ST_NEGMU), c(ST_TMP3),
                    -1.0 / 2048.0, op=ALU.mult)
            self.fop("vector", "tensor_scalar", c(ST_TMP), c(ST_TMP3),
                     1.0 / (2048.0 * 2048.0), c(ST_TMP3), op0=ALU.mult, op1=ALU.mult)
            self.op("vector", "scalar_tensor_tensor", c(ST_VAR), c(ST_SQ0 + tc),
                    1.0 / 2048.0, c(ST_TMP), op0=ALU.mult, op1=ALU.subtract,
                    incs=[self.inc(cl1)])
            self.op("scalar", "activation", c(ST_SD), c(ST_VAR), AF.Sqrt,
                    bias=f32(1e-5), waits=[(cl1, cl1.n)], incs=[self.inc(cl2)])
            self.wait("vector", cl2, cl2.n)
            self.fop("vector", "reciprocal", c(ST_R), c(ST_SD))
            self.op("vector", "tensor_scalar", y1, y1, c(ST_NEGMU), c(ST_R),
                    op0=ALU.add, op1=ALU.mult, incs=[self.inc(cev)])

    def emit_ffn(self, Q, Eh):
        s = self.sems
        (d1v, d1s, dxp, dxe, dmm, dg, dst, dgm, dmm2, dz, d6v, d6s, dW, dG) = (
            s[k] for k in ("d1v", "d1s", "dxp", "dxe", "dmm", "dg", "dst", "dgm",
                           "dmm2", "dz", "d6v", "d6s", "dW", "dG"))
        r1, r2, r3, r4, r5 = self.r1, self.r2, self.r3, self.r4, self.r5
        pb = self.pb
        c = self.col
        tok0 = Q * QT + Eh * 256

        self.wait("vector", dG, dG.n)
        xp0, xe0 = dxp.n, dxe.n
        ktr = 0
        for tc2 in range(2):
            y1n = r2[:, (Eh * 2 + tc2) * 2048:(Eh * 2 + tc2) * 2048 + 2048]
            A = r1[:, 0:2048]
            B = r1[:, 2048:4096]
            self.fop("vector", "reduce_sum", c(ST_TMP), y1n, axis=AX.X, incs=[self.inc(d1v)])
            self.op("scalar", "activation", r1[:, 4096:6144], y1n, AF.Square,
                    accum_out=c(ST_MSQ), waits=[(d1v, d1v.n)], incs=[self.inc(d1s)])
            self.wait("vector", d1s, d1s.n)
            self.op("vector", "tensor_single_scalar", c(ST_NEGMU), c(ST_TMP), -1.0 / 2048.0, op=ALU.mult)
            self.fop("vector", "tensor_scalar", c(ST_TMP2), c(ST_TMP),
                     1.0 / (2048.0 * 2048.0), c(ST_TMP), op0=ALU.mult, op1=ALU.mult)
            self.op("vector", "scalar_tensor_tensor", c(ST_VAR), c(ST_MSQ),
                    1.0 / 2048.0, c(ST_TMP2), op0=ALU.mult, op1=ALU.subtract,
                    incs=[self.inc(d1v)])
            self.op("scalar", "activation", c(ST_SD), c(ST_VAR), AF.Sqrt, bias=f32(1e-6),
                    waits=[(d1v, d1v.n)], incs=[self.inc(d1s)])
            self.wait("vector", d1s, d1s.n)
            self.fop("vector", "reciprocal", c(ST_R), c(ST_SD))
            self.op("vector", "tensor_scalar", A, y1n, c(ST_NEGMU), c(ST_R),
                    op0=ALU.add, op1=ALU.mult)
            self.fop("vector", "reduce_max", c(ST_GAM), A, axis=AX.X, apply_absolute_value=True)
            self.fop("vector", "reciprocal", c(ST_RCP), c(ST_GAM))
            self.op("vector", "tensor_single_scalar", c(ST_I128), c(ST_RCP), 128.0, op=ALU.mult)
            self.fop("vector", "tensor_single_scalar", c(ST_GD128), c(ST_GAM), 0.0078125, op=ALU.mult)
            self.op("vector", "tensor_scalar", B, A, c(ST_I128), c(SC_BPOS),
                    op0=ALU.mult, op1=ALU.min)
            self.op("vector", "tensor_scalar", A, B, c(SC_BNEG), c(ST_GD128),
                    op0=ALU.max, op1=ALU.mult, incs=[self.inc(d1v)])
            self.wait("tensor", d1v, d1v.n)
            for ec in range(16):
                wt = []
                if ktr >= 2:
                    wt.append((dxe, xe0 + ktr - 1))
                self.op("tensor", "transpose", pb[4 + ktr % 2][:, 0:128],
                        A[:, ec * 128:ec * 128 + 128], self.ident, waits=wt,
                        incs=[self.inc(dxp)])
                self.op("vector", "tensor_copy",
                        r5[:, ec * 256 + tc2 * 128:ec * 256 + tc2 * 128 + 128],
                        pb[4 + ktr % 2][:, 0:128], waits=[(dxp, dxp.n)],
                        incs=[self.inc(dxe)])
                ktr += 1

        if Q == 0 and Eh == 0:
            self.dump("dbg_d1", r1[:, :])
            self.dump("dbg_st1", self.stt[:, :])

        dW0, mm0, g0 = dW.n, dmm.n, dg.n
        for og in range(32):
            wt = []
            if og == 0:
                wt.append((dG, self.tern_dG))
            if og >= 2:
                wt.append((dmm, mm0 + og - 1))
            self.op("sync", "dma_start", waits=wt, incs=[self.inc(dW, 16)],
                    out=r4[:, (og % 2) * 4096:(og % 2) * 4096 + 4096].rearrange(
                        "p (c o) -> p c o", c=16),
                    in_=self.hb["sfc"].rearrange("(c p) o -> p c o", p=128)[:, :, og * 256:og * 256 + 256])
            strip = r4[:, (og % 2) * 4096:(og % 2) * 4096 + 4096]
            self.wait("tensor", dW, dW0 + 16 * min(og + 2, 32))
            self.wait("tensor", dxe, dxe.n)
            if og >= 1:
                self.wait("tensor", dg, g0 + 2 * og)
            for ec in range(16):
                for tc2 in range(2):
                    self.op("tensor", "matmul", pb[tc2][:, 0:256],
                            r5[:, ec * 256 + tc2 * 128:ec * 256 + tc2 * 128 + 128],
                            strip[:, ec * 256:ec * 256 + 256],
                            start=(ec == 0), stop=(ec == 15),
                            incs=[self.inc(dmm)] if (ec == 15 and tc2 == 1) else [])
            for tc2 in range(2):
                self.op("scalar", "activation",
                        r3[:, tc2 * 8192 + og * 256:tc2 * 8192 + og * 256 + 256],
                        pb[tc2][:, 0:256], AF.Gelu,
                        waits=[(dmm, mm0 + og + 1)], incs=[self.inc(dg)])

        if Q == 0 and Eh == 0:
            self.dump("dbg_xh", self.r5[:, :])
            self.dump("dbg_gel", r3[:, :])

        self.wait("vector", dg, dg.n)
        self.wait("gpsimd", dg, dg.n)
        for tc2 in range(2):
            g = r3[:, tc2 * 8192:tc2 * 8192 + 8192]
            self.op("vector", "reduce_sum", c(ST_TMP), g, axis=AX.X)
            self.op("vector", "reduce_max", c(ST_GMAX), g, axis=AX.X, incs=[self.inc(d1v)])
            self.fop("vector", "tensor_reduce", c(ST_GMIN), g, axis=AX.X, op=ALU.min,
                     incs=[self.inc(dgm)])
            sq_wt = [(dst, dst.n)] if tc2 == 1 else []
            self.op("scalar", "activation", r1[:, 0:8192], g, AF.Square,
                    accum_out=c(ST_MSQ), waits=sq_wt, incs=[self.inc(d1s)])
            self.wait("vector", d1s, d1s.n)
            self.op("vector", "tensor_single_scalar", c(ST_NEGMU), c(ST_TMP), -1.0 / 8192.0, op=ALU.mult)
            self.op("vector", "tensor_single_scalar", c(ST_MU), c(ST_TMP), 1.0 / 8192.0, op=ALU.mult)
            self.fop("vector", "tensor_scalar", c(ST_TMP2), c(ST_TMP),
                     1.0 / (8192.0 * 8192.0), c(ST_TMP), op0=ALU.mult, op1=ALU.mult)
            self.op("vector", "scalar_tensor_tensor", c(ST_VAR), c(ST_MSQ),
                    1.0 / 8192.0, c(ST_TMP2), op0=ALU.mult, op1=ALU.subtract,
                    incs=[self.inc(d1v)])
            self.op("scalar", "activation", c(ST_SD), c(ST_VAR), AF.Sqrt, bias=f32(1e-6),
                    waits=[(d1v, d1v.n)], incs=[self.inc(d1s)])
            self.wait("vector", d1s, d1s.n)
            self.fop("vector", "reciprocal", c(ST_R), c(ST_SD))
            self.wait("vector", dgm, dgm.n)
            self.op("vector", "tensor_tensor", c(ST_GA), c(ST_GMAX), c(ST_MU), op=ALU.subtract)
            self.fop("vector", "tensor_tensor", c(ST_GB), c(ST_MU), c(ST_GMIN), op=ALU.subtract)
            self.fop("vector", "tensor_tensor", c(ST_GM), c(ST_GA), c(ST_GB), op=ALU.max)
            self.fop("vector", "tensor_tensor", c(ST_GAM), c(ST_GM), c(ST_R), op=ALU.mult)
            self.fop("vector", "reciprocal", c(ST_RCP), c(ST_GAM))
            self.op("vector", "tensor_single_scalar", c(ST_I128), c(ST_RCP), 128.0, op=ALU.mult)
            self.fop("vector", "tensor_single_scalar", c(ST_GD128), c(ST_GAM), 0.0078125, op=ALU.mult)
            self.op("vector", "tensor_scalar", r1[:, 0:8192], g, c(ST_NEGMU), c(ST_R),
                    op0=ALU.add, op1=ALU.mult)
            self.op("vector", "tensor_scalar", r4[:, 0:8192], r1[:, 0:8192],
                    c(ST_I128), c(SC_BPOS), op0=ALU.mult, op1=ALU.min)
            self.op("vector", "tensor_scalar", g, r4[:, 0:8192], c(SC_BNEG), c(ST_GD128),
                    op0=ALU.max, op1=ALU.mult, incs=[self.inc(dst)])

        if Q == 0 and Eh == 0:
            self.dump("dbg_st", self.stt[:, :])

        xp1, xe1 = dxp.n, dxe.n
        ktr = 0
        self.wait("tensor", dst, dst.n)
        for tc2 in range(2):
            for oc in range(64):
                tile_ap = r3[:, tc2 * 8192 + oc * 128:tc2 * 8192 + oc * 128 + 128]
                wt = []
                if ktr >= 2:
                    wt.append((dxe, xe1 + ktr - 1))
                self.op("tensor", "transpose", pb[4 + ktr % 2][:, 0:128], tile_ap,
                        self.ident, waits=wt, incs=[self.inc(dxp)])
                self.op("vector", "tensor_copy", tile_ap, pb[4 + ktr % 2][:, 0:128],
                        waits=[(dxp, dxp.n)], incs=[self.inc(dxe)])
                ktr += 1

        dW1, m20, z0 = dW.n, dmm2.n, dz.n
        ocb_g = 0
        for eg in range(4):
            for ocb in range(8):
                wt = [(dst, dst.n)]
                if ocb_g == 0:
                    wt.append((dG, self.tern_dG))
                if ocb_g >= 2:
                    wt.append((dmm2, m20 + ocb_g - 1))
                self.op("sync", "dma_start", waits=wt, incs=[self.inc(dW, 16)],
                        out=r4[:, (ocb_g % 2) * 4096:(ocb_g % 2) * 4096 + 4096].rearrange(
                            "p (c o) -> p c o", c=8),
                        in_=self.hb["spj"].rearrange("(c p) o -> p c o", p=128)[
                            :, ocb * 8:ocb * 8 + 8, eg * 512:eg * 512 + 512])
                strip = r4[:, (ocb_g % 2) * 4096:(ocb_g % 2) * 4096 + 4096]
                self.wait("tensor", dW, dW1 + 16 * min(ocb_g + 2, 32))
                if ocb == 0 and eg > 0:
                    self.wait("tensor", dz, z0 + 2 * eg)
                self.wait("tensor", dxe, dxe.n)
                for oc8 in range(8):
                    oc = ocb * 8 + oc8
                    for tc2 in range(2):
                        self.op("tensor", "matmul", pb[tc2][:, 0:512],
                                r3[:, tc2 * 8192 + oc * 128:tc2 * 8192 + oc * 128 + 128],
                                strip[:, oc8 * 512:oc8 * 512 + 512],
                                start=(oc == 0), stop=(oc == 63),
                                incs=[self.inc(dmm2)] if (oc8 == 7 and tc2 == 1) else [])
                ocb_g += 1
            self.wait("vector", dmm2, m20 + 8 * (eg + 1))
            for tc2 in range(2):
                y = r2[:, (Eh * 2 + tc2) * 2048 + eg * 512:(Eh * 2 + tc2) * 2048 + eg * 512 + 512]
                self.op("vector", "scalar_tensor_tensor", y, pb[tc2][:, 0:512], 1.0, y,
                        op0=ALU.mult, op1=ALU.add, incs=[self.inc(dz)])

        for tc2 in range(2):
            y = r2[:, (Eh * 2 + tc2) * 2048:(Eh * 2 + tc2) * 2048 + 2048]
            stg = r5[:, tc2 * 2048:tc2 * 2048 + 2048]
            self.op("scalar", "activation", r1[:, 0:2048], y, AF.Square,
                    accum_out=c(ST_SQ0 + tc2), waits=[(dz, dz.n)], incs=[self.inc(d6s)])
            self.fop("vector", "reduce_sum", c(ST_TMP3), y, axis=AX.X,
                     waits=[(dz, dz.n)])
            self.wait("vector", d6s, d6s.n)
            self.op("vector", "tensor_single_scalar", c(ST_NEGMU), c(ST_TMP3),
                    -1.0 / 2048.0, op=ALU.mult)
            self.fop("vector", "tensor_scalar", c(ST_TMP2), c(ST_TMP3),
                     1.0 / (2048.0 * 2048.0), c(ST_TMP3), op0=ALU.mult, op1=ALU.mult)
            self.op("vector", "scalar_tensor_tensor", c(ST_VAR), c(ST_SQ0 + tc2),
                    1.0 / 2048.0, c(ST_TMP2), op0=ALU.mult, op1=ALU.subtract,
                    incs=[self.inc(d6v)])
            self.op("scalar", "activation", c(ST_SD), c(ST_VAR), AF.Sqrt, bias=f32(1e-5),
                    waits=[(d6v, d6v.n)], incs=[self.inc(d6s)])
            self.wait("vector", d6s, d6s.n)
            self.fop("vector", "reciprocal", c(ST_R), c(ST_SD))
            self.op("vector", "tensor_scalar", stg, y, c(ST_NEGMU), c(ST_R),
                    op0=ALU.add, op1=ALU.mult, incs=[self.inc(d6v)])
            self.op("scalar", "dma_start", waits=[(d6v, d6v.n)], incs=[self.inc(dG, 16)],
                    out=self.hb["y"][tok0 + tc2 * 128:tok0 + tc2 * 128 + 128, :], in_=stg)

    def emit_finish(self):
        dG = self.sems["dG"]
        self.wait("gpsimd", dG, dG.n)
        self.rec.flush()


def build_program(dbg=(), skip_tern=False):
    nc = bass.Bass("TRN2", target_bir_lowering=False, dynamic_dma_scratch_size=4096,
                   detect_race_conditions=False)
    kb = KB(nc, dbg=dbg)
    kb.build(skip_tern=skip_tern)
    return nc, kb


def kernel(L0_attn_x, L0_attn_Wq, L0_attn_Wk, L0_attn_Wv, L0_attn_Wo,
           L0_ffn_w_fc, L0_ffn_w_proj):
    x = np.ascontiguousarray(np.asarray(L0_attn_x, dtype=np.float32))
    wqT = np.ascontiguousarray(np.asarray(L0_attn_Wq, dtype=np.float32).T)
    wkT = np.ascontiguousarray(np.asarray(L0_attn_Wk, dtype=np.float32).T)
    wvT = np.ascontiguousarray(np.asarray(L0_attn_Wv, dtype=np.float32).T)
    woT = np.ascontiguousarray(np.asarray(L0_attn_Wo, dtype=np.float32).T)
    wfcT = np.ascontiguousarray(np.asarray(L0_ffn_w_fc, dtype=np.float32).T)
    wpjT = np.ascontiguousarray(np.asarray(L0_ffn_w_proj, dtype=np.float32).T)
    consts = host_consts()
    ropet = host_rope()

    nc, kb = build_program()
    in_maps = []
    for cid in range(NCORES):
        in_maps.append({"x": np.ascontiguousarray(x[cid]), "wqT": wqT, "wkT": wkT,
                        "wvT": wvT, "woT": woT, "wfcT": wfcT, "wpjT": wpjT,
                        "consts": consts, "rope": ropet})
    res = run_bass_kernel_spmd(nc, in_maps, core_ids=list(range(NCORES)))
    out = np.stack([res.results[cid]["y"] for cid in range(NCORES)], axis=0)
    return out.astype(np.float32)


# version 03
# speedup vs baseline: 1.1227x; 1.1227x over previous
import contextlib
import numpy as np

import concourse.bass as bass
import concourse.mybir as mybir
import concourse.bass_utils as _bu
from concourse.bass_utils import run_bass_kernel_spmd

def _verify_and_optimise_no_lint(tmpdir, inp="bir.json", outp="file.neff",
                                 arch=None, *, dve_root=None):
    from pathlib import Path as _Path
    cmd = [
        _bu.get_walrus_driver(),
        "--pass",
        ",".join(["runtime_memory_reservation", "lower_act", "lower_dve",
                  "lower_ap_offset", "codegen", "neff_packager"]),
        "-i", inp,
        "--neff-output-filename", outp,
        "--enable-birsim=true",
        "--mem-mode=physical",
        "--policy=0",
        "--enable-ldw-opt=false",
        "--assign-static-dmas-to-sp=false",
        f"--dram-page-size={_bu.aot_getenv('NEURON_SCRATCHPAD_PAGE_SIZE', '256')}",
        f"--enable-neff-debug-info={'false' if _bu.aot_checkenv('CONCOURSE_SCRUB_NEFF_DEBUG_INFO') else 'true'}",
        "--jobs", "8",
        *_bu.get_walrus_args(
            _bu.get_bir_arch(tmpdir, inp) if arch is None else arch,
            tmpdir, dve_root=dve_root),
    ]
    result = _bu.run_command(cmd, cwd=tmpdir)
    if result is not None:
        (_Path(tmpdir) / "log.txt").write_text(result.stdout)
    return f"{tmpdir}/{outp}"


_bu.bir_verify_and_optimise = _verify_and_optimise_no_lint

F32 = mybir.dt.float32
F32R = mybir.dt.float32r
AF = mybir.ActivationFunctionType
ALU = mybir.AluOpType
AX = mybir.AxisListType

T, E, H, KV, D, WIN, DF = 2048, 2048, 16, 4, 128, 512, 8192
NQ, QT, RING = 4, 512, 1024
NCORES = 8
f32 = np.float32

(ST_NEGM, ST_NEGMC, ST_LNL, ST_BIAS2, ST_LVEC, ST_TMP, ST_TMP2, ST_TMP3,
 ST_SUMY0, ST_SUMY1, ST_SUMY2, ST_SUMY3, ST_SQ0, ST_SQ1, ST_SQ2, ST_SQ3,
 ST_MU, ST_NEGMU, ST_MSQ, ST_VAR, ST_SD, ST_R, ST_GMAX, ST_GMIN,
 ST_GA, ST_GB, ST_GM, ST_GAM, ST_I128, ST_GD128, ST_RCP, ST_X31) = range(32)
(SC_BPOS, SC_BNEG, SC_C, SC_SQ, SC_THRF, SC_NTHRF, SC_GAMF,
 SC_THRP, SC_NTHRP, SC_GAMP, SC_T0, SC_T1, SC_T2, SC_T3, SC_T4, SC_T5) = range(32, 48)


def host_consts():
    c = np.zeros((128, 960), dtype=f32)
    c[:, 0:128] = np.eye(128, dtype=f32)
    i = np.arange(128)
    m1 = np.where(i[None, :] <= i[:, None], f32(-1e9), f32(0.0)).astype(f32)
    m2 = np.where(i[None, :] > i[:, None], f32(-1e9), f32(0.0)).astype(f32)
    c[:, 128:256] = m1
    c[:, 256:384] = m2
    c[:, 384:512] = m1.T
    c[:, 512:640] = m2.T
    c[0, 640:768] = 1.0
    c[:, 768] = 1.0
    return c


def host_rope():
    inv_freq = (1.0 / 10000.0 ** (np.arange(0, D, 2, dtype=f32) / f32(D))).astype(f32)
    ang = (np.arange(T, dtype=f32)[:, None] * inv_freq[None, :]).astype(f32)
    cos = np.cos(ang).astype(f32)
    sin = np.sin(ang).astype(f32)
    out = np.zeros((T, 256), dtype=f32)
    out[:, 0:64] = cos
    out[:, 64:128] = cos
    out[:, 128:192] = sin
    out[:, 192:256] = sin
    return out


class Sem:
    def __init__(self, stack, nc, name):
        self.h = stack.enter_context(nc.semaphore(name))
        self.n = 0

    def bump(self, k=1):
        self.n += k
        return self.n


class Rec:

    ENGS = ("tensor", "vector", "scalar", "gpsimd", "sync")

    def __init__(self, block):
        self.block = block
        self.ops = {e: [] for e in self.ENGS}

    def add(self, eng, fn):
        self.ops[eng].append(fn)

    def flush(self):
        for eng in self.ENGS:
            lst = self.ops[eng]
            if not lst:
                continue
            self.ops[eng] = []

            def body(e, _lst=tuple(lst)):
                for f in _lst:
                    f(e)
            getattr(self.block, eng)(body)


class KB:

    def __init__(self, nc, dbg=()):
        self.nc = nc
        self.dbg = set(dbg)
        self.st = contextlib.ExitStack()
        st = self.st

        hb = {}
        hb["x"] = nc.dram_tensor("x", [T, E], F32, kind="ExternalInput").ap()
        hb["wqT"] = nc.dram_tensor("wqT", [E, H * D], F32, kind="ExternalInput").ap()
        hb["wkT"] = nc.dram_tensor("wkT", [E, KV * D], F32, kind="ExternalInput").ap()
        hb["wvT"] = nc.dram_tensor("wvT", [E, KV * D], F32, kind="ExternalInput").ap()
        hb["woT"] = nc.dram_tensor("woT", [H * D, E], F32, kind="ExternalInput").ap()
        hb["wfcT"] = nc.dram_tensor("wfcT", [E, DF], F32, kind="ExternalInput").ap()
        hb["wpjT"] = nc.dram_tensor("wpjT", [DF, E], F32, kind="ExternalInput").ap()
        hb["consts"] = nc.dram_tensor("consts", [128, 960], F32, kind="ExternalInput").ap()
        hb["rope"] = nc.dram_tensor("rope", [T, 256], F32, kind="ExternalInput").ap()
        hb["y"] = nc.dram_tensor("y", [T, E], F32, kind="ExternalOutput").ap()
        hb["sfc"] = nc.dram_tensor("sfc_scr", [E, DF], F32R, kind="Internal").ap()
        hb["spj"] = nc.dram_tensor("spj_scr", [DF, E], F32R, kind="Internal").ap()
        for wn, cols in (("wq", H * D), ("wk", KV * D), ("wv", KV * D), ("wo", E)):
            rows = H * D if wn == "wo" else E
            hb[wn + "_hi"] = nc.dram_tensor(wn + "_hi", [rows, cols], F32R, kind="Internal").ap()
            hb[wn + "_lo"] = nc.dram_tensor(wn + "_lo", [rows, cols], F32R, kind="Internal").ap()
        self.hb = hb
        self.dbg_hbm = {}
        for name, cols in (("dbg_q", 8192), ("dbg_k", 4096), ("dbg_v", 4096),
                           ("dbg_xt", 8192), ("dbg_ot", 8192), ("dbg_y1", 8192),
                           ("dbg_g", 16384), ("dbg_st", 64), ("dbg_pt", 4096),
                           ("dbg_xh", 4096), ("dbg_gel", 16384),
                           ("dbg_d1", 8192), ("dbg_st1", 64), ("dbg_xt1", 8192),
                           ("dbg_at5", 4096), ("dbg_ats", 64)):
            if name in self.dbg:
                self.dbg_hbm[name] = nc.dram_tensor(name, [128, cols], F32, kind="ExternalOutput").ap()

        def sb(name, cols):
            return st.enter_context(nc.sbuf_tensor(name, [128, cols], F32))

        self.cons = sb("cons", 960)
        self.stt = sb("stt", 64)
        self.ropeq = sb("ropeq", 1024)
        self.k_sb = sb("k_sb", KV * RING)
        self.v_sb = sb("v_sb", 8 * 512)
        self.r1 = sb("r1", 8192)
        self.r2 = sb("r2", 8192)
        self.r3 = sb("r3", 16384)
        self.r4 = sb("r4", 8192)
        self.r5 = sb("r5", 4096)

        self.pS = st.enter_context(nc.psum_tensor("pS", [128, 1024], F32))
        self.pb = [st.enter_context(nc.psum_tensor(f"pb{i}", [128, 512], F32)) for i in range(6)]

        self.block = st.enter_context(nc.Block())
        self.rec = Rec(self.block)

        self.sems = {}
        for name in ("dW dA dG dR bar vF sF tnV tnG "
                     "aTx aVx aMM aRp aSt aTq aVe "
                     "b1 bmax be1 blnl bbv bbt bbw bmm2 bmsk2 be2 bpv bov "
                     "cmm cev cl1 cl2 "
                     "d1v d1s dxp dxe dmm dg dst dgm dmm2 dz d6v d6s").split():
            self.sems[name] = Sem(st, nc, "s_" + name)

        self.ident = self.cons[:, 0:128]
        self.m1 = self.cons[:, 128:256]
        self.m2 = self.cons[:, 256:384]
        self.m1t = self.cons[:, 384:512]
        self.m2t = self.cons[:, 512:640]
        self.onesr = self.cons[0:1, 640:768]
        self.onesc = self.cons[:, 768:769]

    def op(self, eng, meth, *args, waits=(), incs=(), **kw):
        def f(e):
            for sem, val in waits:
                e.wait_ge(sem.h, val)
            ins = getattr(e, meth)(*args, **kw)
            for sem, k in incs:
                ins.then_inc(sem.h, k)
        self.rec.add(eng, f)

    def wait(self, eng, sem, val):
        def f(e):
            e.wait_ge(sem.h, val)
        self.rec.add(eng, f)

    def inc(self, sem, k=1):
        sem.bump(k)
        return (sem, k)

    def fop(self, eng, meth, *args, waits=(), incs=(), **kw):
        incs = list(incs)
        if incs:
            self.op(eng, meth, *args, waits=waits, incs=incs, **kw)
            sem = incs[0][0]
            self.wait(eng, sem, sem.n)
        else:
            f = self.sems["vF" if eng == "vector" else "sF"]
            self.op(eng, meth, *args, waits=waits, incs=[self.inc(f)], **kw)
            self.wait(eng, f, f.n)

    def barrier(self):
        self.rec.flush()
        bar = self.sems["bar"]
        tgt = bar.n + 5
        bar.n = tgt
        for eng in Rec.ENGS:
            def f(e, _t=tgt):
                e.nop().then_inc(bar.h, 1)
                e.wait_ge(bar.h, _t)
            self.rec.add(eng, f)
        self.rec.flush()

    def col(self, c):
        return self.stt[:, c:c + 1]

    def dump(self, name, src_ap):
        if name not in self.dbg_hbm:
            return
        self.barrier()
        dG = self.sems["dG"]
        self.op("scalar", "dma_start", incs=[self.inc(dG, 16)],
                out=self.dbg_hbm[name][:, :], in_=src_ap)
        self.wait("scalar", dG, dG.n)
        self.barrier()

    def build(self, skip_tern=False):
        self.emit_setup()
        if not skip_tern:
            self.emit_ternarize("fc")
            self.emit_ternarize("pj")
        self.emit_wsplit()
        self.tern_dG = self.sems["dG"].n
        for Q in range(NQ):
            self.emit_qkv(Q)
            self.barrier()
            if Q == 0:
                self.dump("dbg_xt", self.r3[:, 0:8192])
                self.dump("dbg_q", self.r2[:, :])
                self.dump("dbg_k", self.k_sb[:, :])
                self.dump("dbg_v", self.v_sb[:, :])
            self.emit_attn(Q)
            self.barrier()
            if Q == 0:
                self.dump("dbg_ot", self.r3[:, 0:8192])
            self.emit_wo_ln1(Q)
            self.barrier()
            if Q == 0:
                self.dump("dbg_y1", self.r2[:, :])
            for Eh in range(2):
                self.emit_ffn(Q, Eh)
                self.barrier()
                if Q == 0 and Eh == 0:
                    self.dump("dbg_g", self.r3[:, :])
        self.emit_finish()
        self.rec.flush()

    def emit_setup(self):
        s = self.sems
        dG, d1v, d1s = s["dG"], s["d1v"], s["d1s"]
        if self.dbg:
            for buf in (self.k_sb, self.v_sb, self.r1, self.r2, self.r3, self.stt):
                self.op("vector", "memset", buf[:, :], 0.0)
            self.barrier()
        self.op("sync", "dma_start", incs=[self.inc(dG, 16)],
                out=self.cons[:], in_=self.hb["consts"][:, :])
        w = [(dG, dG.n)]
        c = self.col
        self.op("vector", "tensor_scalar", c(SC_BPOS), self.onesc, 128.0, -1e-6,
                op0=ALU.mult, op1=ALU.add, waits=w)
        self.fop("vector", "tensor_scalar", c(SC_BNEG), self.onesc, -128.0, 1e-6,
                 op0=ALU.mult, op1=ALU.add)
        self.op("vector", "tensor_single_scalar", c(SC_SQ), self.onesc, 128.0,
                op=ALU.mult, incs=[self.inc(d1v)])
        self.op("scalar", "activation", c(SC_SQ), c(SC_SQ), AF.Sqrt,
                waits=[(d1v, d1v.n)], incs=[self.inc(d1s)])
        self.fop("vector", "reciprocal", c(SC_C), c(SC_SQ), waits=[(d1s, d1s.n)])
        self.fop("vector", "tensor_single_scalar", c(SC_T5), c(SC_SQ), -1.0, op=ALU.mult)
        self.barrier()

    def emit_ternarize(self, which):
        s = self.sems
        dW, dG, tnV, tnG = s["dW"], s["dG"], s["tnV"], s["tnG"]
        aTx, aVx = s["aTx"], s["aVx"]
        c = self.col
        if which == "fc":
            w_hbm, s_hbm, rows, cols = self.hb["wfcT"], self.hb["sfc"], E, DF
            thr, nthr, gam, acc = SC_THRF, SC_NTHRF, SC_GAMF, SC_T0
        else:
            w_hbm, s_hbm, rows, cols = self.hb["wpjT"], self.hb["spj"], DF, E
            thr, nthr, gam, acc = SC_THRP, SC_NTHRP, SC_GAMP, SC_T1
        cw = 4096 if cols >= 4096 else cols
        chunks = [(rb, h) for rb in range(rows // 128) for h in range(cols // cw)]
        nch = len(chunks)

        dW0, v0 = dW.n, tnV.n
        self.op("vector", "tensor_single_scalar", c(acc), self.onesc, 0.0, op=ALU.mult)
        for i, (rb, h) in enumerate(chunks):
            wt = [] if i < 2 else [(tnV, v0 + i - 1)]
            self.op("sync", "dma_start", waits=wt, incs=[self.inc(dW, 16)],
                    out=self.r4[:, (i % 2) * 4096:(i % 2) * 4096 + cw],
                    in_=w_hbm[rb * 128:rb * 128 + 128, h * cw:h * cw + cw])
            self.fop("vector", "reduce_sum", c(ST_TMP),
                     self.r4[:, (i % 2) * 4096:(i % 2) * 4096 + cw], axis=AX.X,
                     apply_absolute_value=True,
                     waits=[(dW, dW0 + 16 * min(i + 2, nch))])
            self.op("vector", "tensor_tensor", c(acc), c(acc), c(ST_TMP), op=ALU.add,
                    incs=[self.inc(tnV)])
        self.op("tensor", "matmul", self.pb[4][0:1, 0:1], self.onesc, c(acc),
                start=True, stop=True, waits=[(tnV, tnV.n)], incs=[self.inc(aTx)])
        inv_n = 1.0 / (rows * cols)
        self.fop("vector", "tensor_single_scalar", self.stt[0:1, SC_T2:SC_T2 + 1],
                 self.pb[4][0:1, 0:1], inv_n, op=ALU.mult, waits=[(aTx, aTx.n)])
        self.fop("vector", "tensor_scalar", self.stt[0:1, SC_T3:SC_T3 + 1],
                 self.stt[0:1, SC_T2:SC_T2 + 1], 1e-6, 0.5, op0=ALU.add, op1=ALU.mult)
        self.fop("vector", "tensor_single_scalar", self.stt[0:1, SC_T4:SC_T4 + 1],
                 self.stt[0:1, SC_T3:SC_T3 + 1], -1.0, op=ALU.mult, incs=[self.inc(aVx)])
        self.op("tensor", "matmul", self.pb[5][:, 0:3], self.onesr,
                self.stt[0:1, SC_T2:SC_T2 + 3], start=True, stop=True,
                waits=[(aVx, aVx.n)], incs=[self.inc(aTx)])
        self.op("vector", "tensor_copy", c(gam), self.pb[5][:, 0:1], waits=[(aTx, aTx.n)])
        self.op("vector", "tensor_copy", c(thr), self.pb[5][:, 1:2])
        self.op("vector", "tensor_copy", c(nthr), self.pb[5][:, 2:3])
        self.barrier()

        dW1, dG0 = dW.n, dG.n
        done = []
        for i, (rb, h) in enumerate(chunks):
            eng = "gpsimd" if i % 3 == 2 else "vector"
            sem = tnG if eng == "gpsimd" else tnV
            wsrc = self.r4[:, (i % 2) * 4096:(i % 2) * 4096 + cw]
            if eng == "vector":
                a2, b2 = self.r1[:, 0:cw], self.r1[:, 4096:4096 + cw]
            else:
                a2, b2 = self.r2[:, 0:cw], self.r2[:, 4096:4096 + cw]
            what = self.r3[:, (i % 2) * 4096:(i % 2) * 4096 + cw]
            wt = [] if i < 2 else [done[i - 2]]
            self.op("sync", "dma_start", waits=wt, incs=[self.inc(dW, 16)],
                    out=wsrc, in_=w_hbm[rb * 128:rb * 128 + 128, h * cw:h * cw + cw])
            wt = [(dW, dW1 + 16 * min(i + 2, nch))]
            if i >= 2:
                wt.append((dG, dG0 + 16 * i))
            self.op(eng, "tensor_single_scalar", a2, wsrc, c(thr), op=ALU.is_gt,
                    waits=wt)
            self.op(eng, "tensor_single_scalar", b2, wsrc, c(nthr), op=ALU.is_lt)
            self.op(eng, "tensor_tensor", what.bitcast(F32R), a2, b2, op=ALU.subtract,
                    incs=[self.inc(sem)])
            done.append((sem, sem.n))
            self.op("scalar", "dma_start", waits=[done[i]], incs=[self.inc(dG, 16)],
                    out=s_hbm[rb * 128:rb * 128 + 128, h * cw:h * cw + cw],
                    in_=what.bitcast(F32R))
        self.barrier()

    def emit_wsplit(self):
        s = self.sems
        dW, dG, tnV = s["dW"], s["dG"], s["tnV"]
        jobs = []
        for wn, src in (("wq", "wqT"), ("wk", "wkT"), ("wv", "wvT"), ("wo", "woT")):
            rows = self.hb[src].shape[0]
            cols = self.hb[src].shape[1]
            for rb in range(rows // 128):
                jobs.append((self.hb[src], self.hb[wn + "_hi"], self.hb[wn + "_lo"], rb, cols))
        nch = len(jobs)
        dW0, dG0, v0 = dW.n, dG.n, tnV.n
        for i, (w_hbm, hi_hbm, lo_hbm, rb, cols) in enumerate(jobs):
            wsrc = self.r4[:, (i % 2) * 4096:(i % 2) * 4096 + cols]
            hi = self.r1[:, 0:cols].bitcast(F32R)
            lo = self.r1[:, 4096:4096 + cols].bitcast(F32R)
            wt = [] if i < 2 else [(tnV, v0 + i - 1)]
            self.op("sync", "dma_start", waits=wt, incs=[self.inc(dW, 16)],
                    out=wsrc, in_=w_hbm[rb * 128:rb * 128 + 128, :])
            wt = [(dW, dW0 + 16 * min(i + 2, nch))]
            if i >= 1:
                wt.append((dG, dG0 + 32 * i))
            self.op("vector", "tensor_copy", hi, wsrc, waits=wt)
            self.op("vector", "tensor_tensor", lo, wsrc, hi, op=ALU.subtract,
                    incs=[self.inc(tnV)])
            done = (tnV, tnV.n)
            self.op("scalar", "dma_start", waits=[done], incs=[self.inc(dG, 16)],
                    out=hi_hbm[rb * 128:rb * 128 + 128, :], in_=hi)
            self.op("scalar", "dma_start", incs=[self.inc(dG, 16)],
                    out=lo_hbm[rb * 128:rb * 128 + 128, :], in_=lo)
        self.barrier()

    def emit_qkv(self, Q):
        s = self.sems
        dA, dW = s["dA"], s["dW"]
        aTx, aVx, aMM, aRp, aSt, aTq, aVe = (s[k] for k in
                                             ("aTx", "aVx", "aMM", "aRp", "aSt", "aTq", "aVe"))
        r1, r2, r4, r5 = self.r1, self.r2, self.r4, self.r5
        pb = self.pb
        hb = self.hb

        dA0 = dA.n
        dG = self.sems["dG"]
        self.wait("scalar", dG, dG.n)
        self.wait("vector", dG, dG.n)
        dR = self.sems["dR"]
        for tc in range(4):
            self.op("scalar", "dma_start", incs=[self.inc(dR, 16)],
                    out=self.ropeq[:, tc * 256:tc * 256 + 256],
                    in_=hb["rope"][Q * QT + tc * 128:Q * QT + tc * 128 + 128, :])
        ropeR = dR.n
        aT0, aV0 = aTx.n, aVx.n
        for tc in range(4):
            wt = [] if tc < 2 else [(aTx, aT0 + 16 * (tc - 1))]
            self.op("scalar", "dma_start", waits=wt, incs=[self.inc(dA, 16)],
                    out=r5[:, (tc % 2) * 2048:(tc % 2) * 2048 + 2048],
                    in_=hb["x"][Q * QT + tc * 128:Q * QT + tc * 128 + 128, :])
        k = 0
        for tc in range(4):
            for ec in range(16):
                wt = [(dA, dA0 + 16 * min(tc + 2, 4))]
                if k >= 2:
                    wt.append((aVx, aV0 + k - 1))
                self.op("tensor", "transpose", pb[k % 2][:, 0:128],
                        r5[:, (tc % 2) * 2048 + ec * 128:(tc % 2) * 2048 + ec * 128 + 128],
                        self.ident, waits=wt, incs=[self.inc(aTx)])
                co = ec * 512 + tc * 128
                hi = self.r3[:, co:co + 128].bitcast(F32R)
                self.op("vector", "tensor_copy", hi, pb[k % 2][:, 0:128],
                        waits=[(aTx, aTx.n)])
                self.op("vector", "tensor_tensor",
                        self.r3[:, 8192 + co:8192 + co + 128].bitcast(F32R),
                        pb[k % 2][:, 0:128], hi, op=ALU.subtract, incs=[self.inc(aVx)])
                k += 1
        self.barrier()
        if Q == 0:
            self.dump("dbg_xt1", self.r3[:, 0:8192])

        halves = ([("q", "wq", sq) for sq in range(8)] +
                  [("k", "wk", sk) for sk in range(2)] +
                  [("v", "wv", sv) for sv in range(2)])
        dW0 = dW.n
        aM0, aR0, aS0, aQ0, aE0 = aMM.n, aRp.n, aSt.n, aTq.n, aVe.n
        tile = 0
        for gi, (kind, wn, sidx) in enumerate(halves):
            buf = r4 if gi % 2 == 0 else r1
            wt = [] if gi < 2 else [(aRp, aR0 + 4 * (gi - 1))]
            if Q == 0 and gi == 0:
                wt = [(self.sems["dG"], self.tern_dG)]
            for part, whbm in ((0, hb[wn + "_hi"]), (1, hb[wn + "_lo"])):
                self.op("sync", "dma_start", waits=wt, incs=[self.inc(dW, 16)],
                        out=buf[:, part * 4096:part * 4096 + 4096].bitcast(F32R).rearrange(
                            "p (c o) -> p c o", c=16),
                        in_=whbm.rearrange("(c p) o -> p c o", p=128)[:, :, sidx * 256:sidx * 256 + 256])
                wt = []
            self.wait("tensor", dW, dW0 + 32 * min(gi + 2, len(halves)))
            if gi >= 1:
                self.wait("tensor", aRp, aR0 + 4 * gi)
            for ec in range(16):
                whiv = buf[:, ec * 256:ec * 256 + 256].bitcast(F32R)
                wlov = buf[:, 4096 + ec * 256:4096 + ec * 256 + 256].bitcast(F32R)
                for tc in range(4):
                    co = ec * 512 + tc * 128
                    xhi = self.r3[:, co:co + 128].bitcast(F32R)
                    xlo = self.r3[:, 8192 + co:8192 + co + 128].bitcast(F32R)
                    for p3, (lh, rh) in enumerate(((xhi, whiv), (xhi, wlov), (xlo, whiv))):
                        last = (ec == 15 and p3 == 2)
                        self.op("tensor", "matmul", pb[tc][:, 0:256], lh, rh,
                                start=(ec == 0 and p3 == 0), stop=last,
                                incs=[self.inc(aMM)] if last else [])
            for tc in range(4):
                mm_t = aM0 + gi * 4 + tc + 1
                if kind == "v":
                    slot = (Q * 4 + tc) % 8
                    self.op("vector", "tensor_copy",
                            self.v_sb[:, slot * 512 + sidx * 256:slot * 512 + sidx * 256 + 256],
                            pb[tc][:, 0:256], waits=[(aMM, mm_t)], incs=[self.inc(aRp)])
                    continue
                pq = pb[tc][:, 0:256].rearrange("p (h x) -> p h x", h=2)
                lo, hi = pq[:, :, 0:64], pq[:, :, 64:128]
                cosr = self.ropeq[:, tc * 256:tc * 256 + 128].rearrange("p (h x) -> p h x", h=2)
                sinr = self.ropeq[:, tc * 256 + 128:tc * 256 + 256].rearrange("p (h x) -> p h x", h=2)
                stg = r5[:, (tile % 2) * 256:(tile % 2) * 256 + 256]
                stgv = stg.rearrange("p (h x) -> p h x", h=2)
                t1 = r5[:, 512:640].rearrange("p (h x) -> p h x", h=2)
                t2 = r5[:, 640:768].rearrange("p (h x) -> p h x", h=2)
                wt = [(aMM, mm_t), (self.sems["dR"], ropeR)]
                if tile >= 2:
                    wt.append((aTq, aQ0 + 2 * (tile - 1)))
                self.op("vector", "tensor_tensor", t1, lo, cosr, op=ALU.mult, waits=wt)
                self.op("vector", "tensor_tensor", t2, hi, sinr, op=ALU.mult)
                self.op("vector", "tensor_tensor", stgv[:, :, 0:64], t1, t2, op=ALU.subtract)
                self.op("vector", "tensor_tensor", t1, hi, cosr, op=ALU.mult)
                self.op("vector", "tensor_tensor", t2, lo, sinr, op=ALU.mult,
                        incs=[self.inc(aRp)])
                self.op("vector", "tensor_tensor", stgv[:, :, 64:128], t1, t2, op=ALU.add,
                        incs=[self.inc(aSt)])
                for hbk in range(2):
                    ktr = 2 * tile + hbk
                    wt = [(aSt, aS0 + tile + 1)]
                    if ktr >= 2:
                        wt.append((aVe, aE0 + ktr - 1))
                    self.op("tensor", "transpose", pb[4 + ktr % 2][:, 0:128],
                            stg[:, hbk * 128:hbk * 128 + 128], self.ident,
                            waits=wt, incs=[self.inc(aTq)])
                    if kind == "q":
                        hglob = sidx * 2 + hbk
                        dst = r2[:, hglob * 512 + tc * 128:hglob * 512 + tc * 128 + 128]
                    else:
                        kvh = sidx * 2 + hbk
                        rpos = (Q % 2) * 512 + tc * 128
                        dst = self.k_sb[:, kvh * RING + rpos:kvh * RING + rpos + 128]
                    self.op("vector", "tensor_copy", dst, pb[4 + ktr % 2][:, 0:128],
                            waits=[(aTq, aTq.n)], incs=[self.inc(aVe)])
                tile += 1

    def emit_attn(self, Q):
        s = self.sems
        (b1, bmax, be1, blnl, bbv, bbt, bbw, bmm2, bmsk2, be2, bpv, bov) = (
            s[k] for k in ("b1", "bmax", "be1", "blnl", "bbv", "bbt", "bbw",
                           "bmm2", "bmsk2", "be2", "bpv", "bov"))
        dA = s["dA"]
        r2, r3, r5 = self.r2, self.r3, self.r5
        pS, pb = self.pS, self.pb
        c = self.col
        scrP = r5[:, 0:640]
        PT = [r5[:, 640:1280], r5[:, 1280:1920]]
        brow = [r5[0:1, 1920:2048], r5[0:1, 2048:2176]]

        for tc in range(4):
            self.op("scalar", "dma_start", incs=[self.inc(dA, 16)],
                    out=self.r4[:, tc * 2048:tc * 2048 + 2048],
                    in_=self.hb["x"][Q * QT + tc * 128:Q * QT + tc * 128 + 128, :])

        i0 = {k: s[k].n for k in ("b1", "bmax", "be1", "blnl", "bbv", "bbt", "bbw", "bpv", "bov")}
        c20, cm0, ce0 = bmm2.n, bmsk2.n, be2.n
        c2 = 0
        for i, (h, qt) in enumerate([(h, qt) for h in range(H) for qt in range(4)]):
            kvh = h // 4
            qs = Q * QT + qt * 128
            win_lo = max(0, qs - WIN)
            L = qs - win_lo
            qtile = r2[:, h * 512 + qt * 128:h * 512 + qt * 128 + 128]

            segs = []
            if L > 0:
                r0 = win_lo % RING
                p1 = min(RING - r0, L)
                segs.append((0, r0, p1))
                if p1 < L:
                    segs.append((p1, 0, L - p1))
            wt = [(be1, i0["be1"] + i)] if i > 0 else []
            first = True
            for si, (off, rpos, ln) in enumerate(segs):
                self.op("tensor", "matmul", pS[:, off:off + ln], qtile,
                        self.k_sb[:, kvh * RING + rpos:kvh * RING + rpos + ln],
                        start=first, stop=False, waits=wt)
                wt = []
                first = False
            dpos = qs % RING
            self.op("tensor", "matmul", pS[:, L:L + 128], qtile,
                    self.k_sb[:, kvh * RING + dpos:kvh * RING + dpos + 128],
                    start=(L == 0 or L == WIN), stop=False, waits=wt)
            if L == WIN:
                self.op("tensor", "matmul", pS[:, 0:128], self.ident, self.m1,
                        start=False, stop=True)
                self.op("tensor", "matmul", pS[:, L:L + 128], self.ident, self.m2,
                        start=False, stop=True, incs=[self.inc(b1)])
            else:
                self.op("tensor", "matmul", pS[:, L:L + 128], self.ident, self.m2,
                        start=False, stop=True, incs=[self.inc(b1)])
            self.wait("vector", b1, i0["b1"] + i + 1)
            self.fop("vector", "reduce_max", c(ST_NEGM), pS[:, 0:L + 128], axis=AX.X, negate=True)
            self.op("vector", "tensor_tensor", c(ST_NEGMC), c(ST_NEGM), c(SC_C), op=ALU.mult,
                    incs=[self.inc(bmax)])
            self.wait("scalar", bmax, i0["bmax"] + i + 1)
            self.op("scalar", "activation", scrP[:, 0:L + 128], pS[:, 0:L + 128], AF.Exp,
                    bias=c(ST_NEGMC), scale=c(SC_C), accum_out=c(ST_LVEC),
                    incs=[self.inc(be1)])
            self.wait("scalar", be1, be1.n)
            self.op("scalar", "activation", c(ST_LNL), c(ST_LVEC), AF.Ln, incs=[self.inc(blnl)])
            self.wait("vector", blnl, i0["blnl"] + i + 1)
            self.op("vector", "scalar_tensor_tensor", c(ST_BIAS2), c(ST_LNL), c(SC_T5),
                    c(ST_NEGM), op0=ALU.mult, op1=ALU.add, incs=[self.inc(bbv)])
            wt = [(bbv, i0["bbv"] + i + 1)]
            if i >= 2:
                wt.append((bbw, i0["bbw"] + i - 1))
            self.op("tensor", "transpose", pb[4 + i % 2][0:1, 0:128], c(ST_BIAS2),
                    self.ident, waits=wt, incs=[self.inc(bbt)])
            self.op("vector", "tensor_copy", brow[i % 2], pb[4 + i % 2][0:1, 0:128],
                    waits=[(bbt, i0["bbt"] + i + 1)], incs=[self.inc(bbw)])

            nch = L // 128 + 1
            for ct in range(nch):
                kpos = win_lo + ct * 128
                rpos = kpos % RING
                bank = pb[c2 % 2]
                wt = [(bbw, i0["bbw"] + i + 1)]
                if c2 >= 2:
                    wt.append((be2, ce0 + c2 - 1))
                msk = None
                if ct == nch - 1:
                    msk = self.m2t
                elif ct == 0 and L == WIN:
                    msk = self.m1t
                self.op("tensor", "matmul", bank[:, 0:128],
                        self.k_sb[:, kvh * RING + rpos:kvh * RING + rpos + 128],
                        qtile, start=True, stop=False, waits=wt)
                if msk is not None:
                    self.op("tensor", "matmul", bank[:, 0:128], self.ident, msk,
                            start=False, stop=False)
                self.op("tensor", "matmul", bank[:, 0:128], self.onesr, brow[i % 2],
                        start=False, stop=True, incs=[self.inc(bmm2)])
                wt = [(bmm2, c20 + c2 + 1)]
                if ct == 0 and i >= 2:
                    wt.append((bpv, i0["bpv"] + i - 1))
                self.op("scalar", "activation", PT[i % 2][:, ct * 128:ct * 128 + 128],
                        bank[:, 0:128], AF.Exp, scale=c(SC_C),
                        waits=wt, incs=[self.inc(be2)])
                slot = (kpos // 128) % 8
                wt = [(be2, ce0 + c2 + 1)]
                if ct == 0 and i >= 2:
                    wt.append((bov, i0["bov"] + i - 1))
                self.op("tensor", "matmul", pb[2 + i % 2][:, 0:128],
                        self.v_sb[:, slot * 512 + kvh * 128:slot * 512 + kvh * 128 + 128],
                        PT[i % 2][:, ct * 128:ct * 128 + 128],
                        start=(ct == 0), stop=(ct == nch - 1), waits=wt,
                        incs=[self.inc(bpv)] if ct == nch - 1 else [])
                c2 += 1
            co = h * 512 + qt * 128
            ohi = r3[:, co:co + 128].bitcast(F32R)
            self.op("vector", "tensor_copy", ohi, pb[2 + i % 2][:, 0:128],
                    waits=[(bpv, i0["bpv"] + i + 1)])
            self.op("vector", "tensor_tensor",
                    r3[:, 8192 + co:8192 + co + 128].bitcast(F32R),
                    pb[2 + i % 2][:, 0:128], ohi, op=ALU.subtract, incs=[self.inc(bov)])
            if Q == 0 and i == 0:
                self.dump("dbg_at5", r5[:, :])
                self.dump("dbg_ats", self.stt[:, :])

    def emit_wo_ln1(self, Q):
        s = self.sems
        dW, dA, cmm, cev, cl1, cl2 = (s[k] for k in ("dW", "dA", "cmm", "cev", "cl1", "cl2"))
        r2, r3, r4, r5 = self.r2, self.r3, self.r4, self.r5
        pb = self.pb
        c = self.col
        dW0, cm0, ce0 = dW.n, cmm.n, cev.n
        widx = 0
        for eg in range(4):
            for hd in range(16):
                wt = []
                if Q == 0 and widx == 0:
                    wt.append((self.sems["dG"], self.tern_dG))
                if widx >= 2:
                    wt.append((cmm, cm0 + widx - 1))
                base = (widx % 2) * 1024
                for part, whbm in ((0, self.hb["wo_hi"]), (1, self.hb["wo_lo"])):
                    self.op("sync", "dma_start", waits=wt, incs=[self.inc(dW, 16)],
                            out=r5[:, base + part * 512:base + part * 512 + 512].bitcast(F32R),
                            in_=whbm[hd * 128:hd * 128 + 128, eg * 512:eg * 512 + 512])
                    wt = []
                wt = [(dW, dW0 + 32 * min(widx + 2, 64))]
                if hd == 0 and eg > 0:
                    wt.append((cev, ce0 + 4 * eg))
                whiv = r5[:, base:base + 512].bitcast(F32R)
                wlov = r5[:, base + 512:base + 1024].bitcast(F32R)
                for tc in range(4):
                    co = hd * 512 + tc * 128
                    ohi = r3[:, co:co + 128].bitcast(F32R)
                    olo = r3[:, 8192 + co:8192 + co + 128].bitcast(F32R)
                    for p3, (lh, rh) in enumerate(((ohi, whiv), (ohi, wlov), (olo, whiv))):
                        self.op("tensor", "matmul", pb[tc][:, 0:512], lh, rh,
                                start=(hd == 0 and p3 == 0), stop=(hd == 15 and p3 == 2),
                                waits=wt if (tc == 0 and p3 == 0) else [],
                                incs=[self.inc(cmm)] if (tc == 3 and p3 == 2) else [])
                widx += 1
            self.wait("vector", cmm, cm0 + widx)
            self.wait("vector", dA, dA.n)
            for tc in range(4):
                self.op("vector", "scalar_tensor_tensor",
                        r2[:, tc * 2048 + eg * 512:tc * 2048 + eg * 512 + 512],
                        pb[tc][:, 0:512], 1.0,
                        r4[:, tc * 2048 + eg * 512:tc * 2048 + eg * 512 + 512],
                        op0=ALU.mult, op1=ALU.add, incs=[self.inc(cev)])
        for tc in range(4):
            y1 = r2[:, tc * 2048:tc * 2048 + 2048]
            self.op("scalar", "activation", r5[:, 2048:4096], y1, AF.Square,
                    accum_out=c(ST_SQ0 + tc), waits=[(cev, cev.n)], incs=[self.inc(cl2)])
            self.fop("vector", "reduce_sum", c(ST_TMP3), y1, axis=AX.X,
                     waits=[(cev, cev.n)])
            self.wait("vector", cl2, cl2.n)
            self.op("vector", "tensor_single_scalar", c(ST_NEGMU), c(ST_TMP3),
                    -1.0 / 2048.0, op=ALU.mult)
            self.fop("vector", "tensor_scalar", c(ST_TMP), c(ST_TMP3),
                     1.0 / (2048.0 * 2048.0), c(ST_TMP3), op0=ALU.mult, op1=ALU.mult)
            self.op("vector", "scalar_tensor_tensor", c(ST_VAR), c(ST_SQ0 + tc),
                    1.0 / 2048.0, c(ST_TMP), op0=ALU.mult, op1=ALU.subtract,
                    incs=[self.inc(cl1)])
            self.op("scalar", "activation", c(ST_SD), c(ST_VAR), AF.Sqrt,
                    bias=f32(1e-5), waits=[(cl1, cl1.n)], incs=[self.inc(cl2)])
            self.wait("vector", cl2, cl2.n)
            self.fop("vector", "reciprocal", c(ST_R), c(ST_SD))
            self.op("vector", "tensor_scalar", y1, y1, c(ST_NEGMU), c(ST_R),
                    op0=ALU.add, op1=ALU.mult, incs=[self.inc(cev)])

    def emit_ffn(self, Q, Eh):
        s = self.sems
        (d1v, d1s, dxp, dxe, dmm, dg, dst, dgm, dmm2, dz, d6v, d6s, dW, dG) = (
            s[k] for k in ("d1v", "d1s", "dxp", "dxe", "dmm", "dg", "dst", "dgm",
                           "dmm2", "dz", "d6v", "d6s", "dW", "dG"))
        r1, r2, r3, r4, r5 = self.r1, self.r2, self.r3, self.r4, self.r5
        pb = self.pb
        c = self.col
        tok0 = Q * QT + Eh * 256

        self.wait("vector", dG, dG.n)
        xp0, xe0 = dxp.n, dxe.n
        ktr = 0
        for tc2 in range(2):
            y1n = r2[:, (Eh * 2 + tc2) * 2048:(Eh * 2 + tc2) * 2048 + 2048]
            A = r1[:, 0:2048]
            B = r1[:, 2048:4096]
            self.fop("vector", "reduce_sum", c(ST_TMP), y1n, axis=AX.X, incs=[self.inc(d1v)])
            self.op("scalar", "activation", r3[:, 0:2048], y1n, AF.Square,
                    accum_out=c(ST_MSQ), waits=[(d1v, d1v.n)], incs=[self.inc(d1s)])
            self.wait("vector", d1s, d1s.n)
            self.op("vector", "tensor_single_scalar", c(ST_NEGMU), c(ST_TMP), -1.0 / 2048.0, op=ALU.mult)
            self.fop("vector", "tensor_scalar", c(ST_TMP2), c(ST_TMP),
                     1.0 / (2048.0 * 2048.0), c(ST_TMP), op0=ALU.mult, op1=ALU.mult)
            self.op("vector", "scalar_tensor_tensor", c(ST_VAR), c(ST_MSQ),
                    1.0 / 2048.0, c(ST_TMP2), op0=ALU.mult, op1=ALU.subtract,
                    incs=[self.inc(d1v)])
            self.op("scalar", "activation", c(ST_SD), c(ST_VAR), AF.Sqrt, bias=f32(1e-6),
                    waits=[(d1v, d1v.n)], incs=[self.inc(d1s)])
            self.wait("vector", d1s, d1s.n)
            self.fop("vector", "reciprocal", c(ST_R), c(ST_SD))
            self.op("vector", "tensor_scalar", A, y1n, c(ST_NEGMU), c(ST_R),
                    op0=ALU.add, op1=ALU.mult)
            self.fop("vector", "reduce_max", c(ST_GAM), A, axis=AX.X, apply_absolute_value=True)
            self.fop("vector", "reciprocal", c(ST_RCP), c(ST_GAM))
            self.op("vector", "tensor_single_scalar", c(ST_I128), c(ST_RCP), 128.0, op=ALU.mult)
            self.fop("vector", "tensor_single_scalar", c(ST_GD128), c(ST_GAM), 0.0078125, op=ALU.mult)
            self.op("vector", "tensor_scalar", B, A, c(ST_I128), c(SC_BPOS),
                    op0=ALU.mult, op1=ALU.min)
            self.op("vector", "tensor_scalar", A, B, c(SC_BNEG), c(ST_GD128),
                    op0=ALU.max, op1=ALU.mult, incs=[self.inc(d1v)])
            self.wait("tensor", d1v, d1v.n)
            for ec in range(16):
                wt = []
                if ktr >= 2:
                    wt.append((dxe, xe0 + ktr - 1))
                self.op("tensor", "transpose", pb[4 + ktr % 2][:, 0:128],
                        A[:, ec * 128:ec * 128 + 128], self.ident, waits=wt,
                        incs=[self.inc(dxp)])
                co = ec * 256 + tc2 * 128
                hi = r5[:, co:co + 128].bitcast(F32R)
                self.op("vector", "tensor_copy", hi, pb[4 + ktr % 2][:, 0:128],
                        waits=[(dxp, dxp.n)])
                self.op("vector", "tensor_tensor",
                        r1[:, 4096 + co:4096 + co + 128].bitcast(F32R),
                        pb[4 + ktr % 2][:, 0:128], hi, op=ALU.subtract,
                        incs=[self.inc(dxe)])
                ktr += 1

        if Q == 0 and Eh == 0:
            self.dump("dbg_d1", r1[:, :])
            self.dump("dbg_st1", self.stt[:, :])

        dW0, mm0, g0 = dW.n, dmm.n, dg.n
        for og in range(32):
            wt = []
            if og == 0:
                wt.append((dG, self.tern_dG))
            if og >= 2:
                wt.append((dmm, mm0 + og - 1))
            self.op("sync", "dma_start", waits=wt, incs=[self.inc(dW, 16)],
                    out=r4[:, (og % 2) * 4096:(og % 2) * 4096 + 4096].bitcast(F32R).rearrange(
                        "p (c o) -> p c o", c=16),
                    in_=self.hb["sfc"].rearrange("(c p) o -> p c o", p=128)[:, :, og * 256:og * 256 + 256])
            self.wait("tensor", dW, dW0 + 16 * min(og + 2, 32))
            self.wait("tensor", dxe, dxe.n)
            if og >= 1:
                self.wait("tensor", dg, g0 + 2 * og)
            for ec in range(16):
                sv = r4[:, (og % 2) * 4096 + ec * 256:(og % 2) * 4096 + ec * 256 + 256].bitcast(F32R)
                for tc2 in range(2):
                    co = ec * 256 + tc2 * 128
                    xhi = r5[:, co:co + 128].bitcast(F32R)
                    xlo = r1[:, 4096 + co:4096 + co + 128].bitcast(F32R)
                    for p2, lh in enumerate((xhi, xlo)):
                        self.op("tensor", "matmul", pb[tc2][:, 0:256], lh, sv,
                                start=(ec == 0 and p2 == 0), stop=(ec == 15 and p2 == 1),
                                incs=[self.inc(dmm)] if (ec == 15 and tc2 == 1 and p2 == 1) else [])
            for tc2 in range(2):
                self.op("scalar", "activation",
                        r3[:, tc2 * 8192 + og * 256:tc2 * 8192 + og * 256 + 256],
                        pb[tc2][:, 0:256], AF.Gelu, scale=c(SC_GAMF),
                        waits=[(dmm, mm0 + og + 1)], incs=[self.inc(dg)])

        if Q == 0 and Eh == 0:
            self.dump("dbg_xh", self.r5[:, :])
            self.dump("dbg_gel", r3[:, :])

        self.wait("vector", dg, dg.n)
        self.wait("gpsimd", dg, dg.n)
        for tc2 in range(2):
            g = r3[:, tc2 * 8192:tc2 * 8192 + 8192]
            self.op("vector", "reduce_sum", c(ST_TMP), g, axis=AX.X)
            self.op("vector", "reduce_max", c(ST_GMAX), g, axis=AX.X, incs=[self.inc(d1v)])
            self.fop("vector", "tensor_reduce", c(ST_GMIN), g, axis=AX.X, op=ALU.min,
                     incs=[self.inc(dgm)])
            sq_wt = [(dst, dst.n)] if tc2 == 1 else []
            self.op("scalar", "activation", r1[:, 0:8192], g, AF.Square,
                    accum_out=c(ST_MSQ), waits=sq_wt, incs=[self.inc(d1s)])
            self.wait("vector", d1s, d1s.n)
            self.op("vector", "tensor_single_scalar", c(ST_NEGMU), c(ST_TMP), -1.0 / 8192.0, op=ALU.mult)
            self.op("vector", "tensor_single_scalar", c(ST_MU), c(ST_TMP), 1.0 / 8192.0, op=ALU.mult)
            self.fop("vector", "tensor_scalar", c(ST_TMP2), c(ST_TMP),
                     1.0 / (8192.0 * 8192.0), c(ST_TMP), op0=ALU.mult, op1=ALU.mult)
            self.op("vector", "scalar_tensor_tensor", c(ST_VAR), c(ST_MSQ),
                    1.0 / 8192.0, c(ST_TMP2), op0=ALU.mult, op1=ALU.subtract,
                    incs=[self.inc(d1v)])
            self.op("scalar", "activation", c(ST_SD), c(ST_VAR), AF.Sqrt, bias=f32(1e-6),
                    waits=[(d1v, d1v.n)], incs=[self.inc(d1s)])
            self.wait("vector", d1s, d1s.n)
            self.fop("vector", "reciprocal", c(ST_R), c(ST_SD))
            self.wait("vector", dgm, dgm.n)
            self.op("vector", "tensor_tensor", c(ST_GA), c(ST_GMAX), c(ST_MU), op=ALU.subtract)
            self.fop("vector", "tensor_tensor", c(ST_GB), c(ST_MU), c(ST_GMIN), op=ALU.subtract)
            self.fop("vector", "tensor_tensor", c(ST_GM), c(ST_GA), c(ST_GB), op=ALU.max)
            self.fop("vector", "tensor_tensor", c(ST_GAM), c(ST_GM), c(ST_R), op=ALU.mult)
            self.fop("vector", "reciprocal", c(ST_RCP), c(ST_GAM))
            self.op("vector", "tensor_single_scalar", c(ST_I128), c(ST_RCP), 128.0, op=ALU.mult)
            self.fop("vector", "tensor_single_scalar", c(ST_GD128), c(ST_GAM), 0.0078125, op=ALU.mult)
            self.op("vector", "tensor_scalar", r1[:, 0:8192], g, c(ST_NEGMU), c(ST_R),
                    op0=ALU.add, op1=ALU.mult)
            self.op("vector", "tensor_scalar", r4[:, 0:8192], r1[:, 0:8192],
                    c(ST_I128), c(SC_BPOS), op0=ALU.mult, op1=ALU.min)
            self.op("vector", "tensor_scalar", g, r4[:, 0:8192], c(SC_BNEG), c(ST_GD128),
                    op0=ALU.max, op1=ALU.mult, incs=[self.inc(dst)])

        if Q == 0 and Eh == 0:
            self.dump("dbg_st", self.stt[:, :])

        xp1, xe1 = dxp.n, dxe.n
        ktr = 0
        dW1, m20, z0 = dW.n, dmm2.n, dz.n
        zacc = r5
        ocb_g = 0
        self.wait("tensor", dst, dst.n)
        for sw in range(2):
            for tc2 in range(2):
                for ocl in range(32):
                    oc = sw * 32 + ocl
                    tile_ap = r3[:, tc2 * 8192 + oc * 128:tc2 * 8192 + oc * 128 + 128]
                    wt = []
                    if ktr >= 2:
                        wt.append((dxe, xe1 + ktr - 1))
                    self.op("tensor", "transpose", pb[4 + ktr % 2][:, 0:128], tile_ap,
                            self.ident, waits=wt, incs=[self.inc(dxp)])
                    hi = tile_ap.bitcast(F32R)
                    self.op("vector", "tensor_copy", hi, pb[4 + ktr % 2][:, 0:128],
                            waits=[(dxp, dxp.n)])
                    self.op("vector", "tensor_tensor",
                            r1[:, tc2 * 4096 + ocl * 128:tc2 * 4096 + ocl * 128 + 128].bitcast(F32R),
                            pb[4 + ktr % 2][:, 0:128], hi, op=ALU.subtract,
                            incs=[self.inc(dxe)])
                    ktr += 1
            for eg in range(4):
                for ocb in range(4):
                    wt = [(dst, dst.n)]
                    if ocb_g == 0:
                        wt.append((dG, self.tern_dG))
                    if ocb_g >= 2:
                        wt.append((dmm2, m20 + ocb_g - 1))
                    self.op("sync", "dma_start", waits=wt, incs=[self.inc(dW, 16)],
                            out=r4[:, (ocb_g % 2) * 4096:(ocb_g % 2) * 4096 + 4096].bitcast(F32R).rearrange(
                                "p (c o) -> p c o", c=8),
                            in_=self.hb["spj"].rearrange("(c p) o -> p c o", p=128)[
                                :, sw * 32 + ocb * 8:sw * 32 + ocb * 8 + 8, eg * 512:eg * 512 + 512])
                    self.wait("tensor", dW, dW1 + 16 * min(ocb_g + 2, 32))
                    if ocb == 0 and (sw > 0 or eg > 0):
                        self.wait("tensor", dz, z0 + 2 * (4 * sw + eg))
                    self.wait("tensor", dxe, dxe.n)
                    for oc8 in range(8):
                        ocl = ocb * 8 + oc8
                        oc = sw * 32 + ocl
                        sv = r4[:, (ocb_g % 2) * 4096 + oc8 * 512:(ocb_g % 2) * 4096 + oc8 * 512 + 512].bitcast(F32R)
                        for tc2 in range(2):
                            xhi = r3[:, tc2 * 8192 + oc * 128:tc2 * 8192 + oc * 128 + 128].bitcast(F32R)
                            xlo = r1[:, tc2 * 4096 + ocl * 128:tc2 * 4096 + ocl * 128 + 128].bitcast(F32R)
                            for p2, lh in enumerate((xhi, xlo)):
                                self.op("tensor", "matmul", pb[tc2][:, 0:512], lh, sv,
                                        start=(ocl == 0 and p2 == 0), stop=(ocl == 31 and p2 == 1),
                                        incs=[self.inc(dmm2)] if (oc8 == 7 and tc2 == 1 and p2 == 1) else [])
                    ocb_g += 1
                self.wait("vector", dmm2, m20 + 16 * sw + 4 * (eg + 1))
                for tc2 in range(2):
                    za = zacc[:, tc2 * 2048 + eg * 512:tc2 * 2048 + eg * 512 + 512]
                    if sw == 0:
                        self.op("vector", "tensor_copy", za, pb[tc2][:, 0:512],
                                incs=[self.inc(dz)])
                    else:
                        y = r2[:, (Eh * 2 + tc2) * 2048 + eg * 512:(Eh * 2 + tc2) * 2048 + eg * 512 + 512]
                        self.op("vector", "tensor_tensor", za, pb[tc2][:, 0:512], za,
                                op=ALU.add)
                        self.op("vector", "scalar_tensor_tensor", y, za, c(SC_GAMP), y,
                                op0=ALU.mult, op1=ALU.add, incs=[self.inc(dz)])

        for tc2 in range(2):
            y = r2[:, (Eh * 2 + tc2) * 2048:(Eh * 2 + tc2) * 2048 + 2048]
            stg = r5[:, tc2 * 2048:tc2 * 2048 + 2048]
            self.op("scalar", "activation", r1[:, 0:2048], y, AF.Square,
                    accum_out=c(ST_SQ0 + tc2), waits=[(dz, dz.n)], incs=[self.inc(d6s)])
            self.fop("vector", "reduce_sum", c(ST_TMP3), y, axis=AX.X,
                     waits=[(dz, dz.n)])
            self.wait("vector", d6s, d6s.n)
            self.op("vector", "tensor_single_scalar", c(ST_NEGMU), c(ST_TMP3),
                    -1.0 / 2048.0, op=ALU.mult)
            self.fop("vector", "tensor_scalar", c(ST_TMP2), c(ST_TMP3),
                     1.0 / (2048.0 * 2048.0), c(ST_TMP3), op0=ALU.mult, op1=ALU.mult)
            self.op("vector", "scalar_tensor_tensor", c(ST_VAR), c(ST_SQ0 + tc2),
                    1.0 / 2048.0, c(ST_TMP2), op0=ALU.mult, op1=ALU.subtract,
                    incs=[self.inc(d6v)])
            self.op("scalar", "activation", c(ST_SD), c(ST_VAR), AF.Sqrt, bias=f32(1e-5),
                    waits=[(d6v, d6v.n)], incs=[self.inc(d6s)])
            self.wait("vector", d6s, d6s.n)
            self.fop("vector", "reciprocal", c(ST_R), c(ST_SD))
            self.op("vector", "tensor_scalar", stg, y, c(ST_NEGMU), c(ST_R),
                    op0=ALU.add, op1=ALU.mult, incs=[self.inc(d6v)])
            self.op("scalar", "dma_start", waits=[(d6v, d6v.n)], incs=[self.inc(dG, 16)],
                    out=self.hb["y"][tok0 + tc2 * 128:tok0 + tc2 * 128 + 128, :], in_=stg)

    def emit_finish(self):
        dG = self.sems["dG"]
        self.wait("gpsimd", dG, dG.n)
        self.rec.flush()


def build_program(dbg=(), skip_tern=False):
    nc = bass.Bass("TRN2", target_bir_lowering=False, dynamic_dma_scratch_size=4096,
                   detect_race_conditions=False)
    kb = KB(nc, dbg=dbg)
    kb.build(skip_tern=skip_tern)
    return nc, kb


def kernel(L0_attn_x, L0_attn_Wq, L0_attn_Wk, L0_attn_Wv, L0_attn_Wo,
           L0_ffn_w_fc, L0_ffn_w_proj):
    x = np.ascontiguousarray(np.asarray(L0_attn_x, dtype=np.float32))
    wqT = np.ascontiguousarray(np.asarray(L0_attn_Wq, dtype=np.float32).T)
    wkT = np.ascontiguousarray(np.asarray(L0_attn_Wk, dtype=np.float32).T)
    wvT = np.ascontiguousarray(np.asarray(L0_attn_Wv, dtype=np.float32).T)
    woT = np.ascontiguousarray(np.asarray(L0_attn_Wo, dtype=np.float32).T)
    wfcT = np.ascontiguousarray(np.asarray(L0_ffn_w_fc, dtype=np.float32).T)
    wpjT = np.ascontiguousarray(np.asarray(L0_ffn_w_proj, dtype=np.float32).T)
    consts = host_consts()
    ropet = host_rope()

    nc, kb = build_program()
    in_maps = []
    for cid in range(NCORES):
        in_maps.append({"x": np.ascontiguousarray(x[cid]), "wqT": wqT, "wkT": wkT,
                        "wvT": wvT, "woT": woT, "wfcT": wfcT, "wpjT": wpjT,
                        "consts": consts, "rope": ropet})
    res = run_bass_kernel_spmd(nc, in_maps, core_ids=list(range(NCORES)))
    out = np.stack([res.results[cid]["y"] for cid in range(NCORES)], axis=0)
    return out.astype(np.float32)
